# version 01
# baseline (speedup 1.0000x reference)
import numpy as np
import ml_dtypes
from contextlib import ExitStack

import concourse.bass as bass
import concourse.mybir as mybir
from concourse.bass_utils import run_bass_kernel_spmd

F32 = mybir.dt.float32
BF16 = mybir.dt.bfloat16
I32 = mybir.dt.int32
AF = mybir.ActivationFunctionType
ALU = mybir.AluOpType
AX = mybir.AxisListType
BF = ml_dtypes.bfloat16

NCORES = 8
D = 1024
TLOC = 2048
ND = 8
SEQ = 8192

ENGS = ("sync", "scalar", "vector", "tensor", "gpsimd")


class Prog:
    def __init__(self, name):
        self.nc = bass.Bass("TRN2", target_bir_lowering=False)
        self.name = name
        self.ops = {e: [] for e in ENGS}
        self.cnt = {}
        self.writer = {}
        self.readers = {}
        self.watermark = {e: {} for e in ENGS}
        self.inline_cache = {}

    NDMASLOT = 8

    def emit(self, eng, fn, reads=(), writes=(), dma=False):
        waits = {}
        wm = self.watermark[eng]

        def need(dep):
            sk, v = dep
            if v > wm.get(sk, 0):
                waits[sk] = max(waits.get(sk, 0), v)

        if dma:
            nissued = self.cnt.get(("dman", eng), 0)
            self.cnt[("dman", eng)] = nissued + 1
            key = ("dma", eng, nissued % self.NDMASLOT)
            need((key, self.cnt.get(key, 0)))
        else:
            key = ("cmp", eng)
        for b in reads:
            w = self.writer.get(b)
            if w:
                need(w)
        for b in writes:
            w = self.writer.get(b)
            if w:
                need(w)
            for sk, v in self.readers.get(b, {}).items():
                need((sk, v))
        inc = 16 if dma else 1
        v = self.cnt.get(key, 0) + inc
        self.cnt[key] = v
        for sk, vv in waits.items():
            wm[sk] = max(wm.get(sk, 0), vv)
        self.ops[eng].append((waits, fn, key, inc, v))
        for b in reads:
            self.readers.setdefault(b, {})[key] = v
        for b in writes:
            self.writer[b] = (key, v)
            self.readers[b] = {}

    def barrier(self):
        snap = {sk: v for sk, v in self.cnt.items() if sk[0] != "dman"}
        engs = ENGS
        for eng in engs:
            waits = {}
            wm = self.watermark[eng]
            for sk, v in snap.items():
                if v > wm.get(sk, 0):
                    waits[sk] = v
                    wm[sk] = v
            key = ("cmp", eng)
            v = self.cnt.get(key, 0) + 1
            self.cnt[key] = v
            self.ops[eng].append((waits, lambda e: e.nop(), key, 1, v))

    def finish(self, tags):
        self.emit("sync", lambda e: e.nop(), reads=tuple(tags))

    def const(self, arr, name):
        if name not in self.inline_cache:
            self.inline_cache[name] = self.nc.inline_tensor(
                np.ascontiguousarray(arr), name)
        return self.inline_cache[name]

    def build(self):
        import bisect
        nc = self.nc
        waited = {}
        for eng in ENGS:
            for waits, fn, key, inc, v in self.ops[eng]:
                for sk, vv in waits.items():
                    waited.setdefault(sk, set()).add(vv)
        ms = {sk: sorted(vs) for sk, vs in waited.items()}

        def rank(sk, vv):
            return bisect.bisect_right(ms[sk], vv)

        used = sorted(ms.keys())
        with ExitStack() as es:
            sems = {sk: es.enter_context(nc.semaphore("_".join(map(str, sk))))
                    for sk in used}
            block = es.enter_context(nc.Block())
            for eng in ENGS:
                if not self.ops[eng]:
                    continue

                def section(engobj, _eng=eng):
                    for waits, fn, key, inc, v in self.ops[_eng]:
                        for sk, vv in sorted(waits.items()):
                            unit = 16 if sk[0] == "dma" else 1
                            engobj.wait_ge(sems[sk], rank(sk, vv) * unit)
                        ins = fn(engobj)
                        if key in ms and v in waited[key]:
                            ins.then_inc(sems[key], inc)

                getattr(block, eng)(section)
        return nc



def shard_fm(xflat):
    return [np.ascontiguousarray(np.asarray(xflat)[2048 * c:2048 * (c + 1)].T)
            for c in range(NCORES)]


def unshard_fm(shards):
    return np.concatenate([np.asarray(s).T for s in shards], axis=0)


def tile_vec(v, n=None):
    v = np.asarray(v, np.float32).reshape(-1)
    return np.ascontiguousarray(v.reshape(-1, 128).T)



def causal_masks_np():
    p = np.arange(128)[:, None]
    f = np.arange(512)[None, :]
    return np.concatenate([np.where(p + r <= f, 0.0, -1e30).astype(np.float32)
                           for r in (0, 128, 256, 384)], axis=1)


def onesblk65_np():
    a = np.zeros((128, 65), np.float32)
    a[0:64, 0] = 1
    a[64:128, 64] = 1
    return a


def onehot8_np():
    a = np.zeros((8, 8 * 128), np.float32)
    for e in range(8):
        a[e, e * 128:(e + 1) * 128] = 1
    return a


def rowsel64_np():
    a = np.zeros((65, 64), BF)
    a[64, :] = 1
    return a


def selmask65x4_np():
    a = np.zeros((65, 4), np.float32)
    for h in range(4):
        a[64 * (h % 2), h] = 1
    return a



def ln_feature_major(p, es, xsb, xtag, T, tile_cb, name):
    nc = p.nc
    oc = es.enter_context(nc.sbuf_tensor(f"{name}_oc", [128, 1], F32))
    o1 = es.enter_context(nc.sbuf_tensor(f"{name}_o1", [1, 128], F32))
    ocd = p.const(np.ones((128, 1), np.float32), "c_onescol_f32")
    o1d = p.const(np.ones((1, 128), np.float32), "c_ones1x128_f32")
    p.emit("sync", lambda e: e.dma_start(out=oc[:], in_=ocd.ap()[:, :]),
           writes=(f"{name}oc",), dma=True)
    p.emit("sync", lambda e: e.dma_start(out=o1[:], in_=o1d.ap()[:, :]),
           writes=(f"{name}o1",), dma=True)
    sq = es.enter_context(nc.sbuf_tensor(f"{name}_sq", [128, T], F32))
    st = es.enter_context(nc.sbuf_tensor(f"{name}_st", [1, 2 * T], F32))
    with nc.psum_tensor(f"{name}_pss", [1, T], F32) as pss, \
         nc.psum_tensor(f"{name}_psq", [1, T], F32) as psq:
        for k in range(ND):
            xa = xsb[:, k * T:(k + 1) * T]
            p.emit("scalar", lambda e, xa=xa: e.activation(out=sq[:, :], in_=xa, func=AF.Square),
                   reads=(xtag,), writes=(f"{name}sq",))
            for tc in range(T // 512):
                sl = slice(tc * 512, (tc + 1) * 512)
                p.emit("tensor", lambda e, xa=xa, sl=sl, k=k: e.matmul(
                    pss[:, sl], oc[:, :], xa[:, sl], start=(k == 0), stop=(k == ND - 1)),
                    reads=(xtag, f"{name}oc"), writes=(f"{name}ps",))
                p.emit("tensor", lambda e, sl=sl, k=k: e.matmul(
                    psq[:, sl], oc[:, :], sq[:, sl], start=(k == 0), stop=(k == ND - 1)),
                    reads=(f"{name}sq", f"{name}oc"), writes=(f"{name}ps",))
        m = st[:, 0:T]
        r = st[:, T:2 * T]
        t2 = sq[0:1, 0:T]
        p.emit("vector", lambda e: e.tensor_scalar(m, pss[:, :], 1.0 / D, None, ALU.mult),
               reads=(f"{name}ps",), writes=(f"{name}m",))
        p.emit("vector", lambda e: e.tensor_mul(t2, m, m),
               reads=(f"{name}m", f"{name}sq"), writes=(f"{name}sq",))
        p.emit("vector", lambda e: e.tensor_scalar(r, psq[:, :], 1.0 / D, None, ALU.mult),
               reads=(f"{name}ps",), writes=(f"{name}r",))
        p.emit("vector", lambda e: e.tensor_sub(r, r, t2),
               reads=(f"{name}r", f"{name}sq"), writes=(f"{name}r",))
        p.emit("vector", lambda e: e.tensor_scalar(r, r, 1e-5, None, ALU.add),
               reads=(f"{name}r",), writes=(f"{name}r",))
        p.emit("scalar", lambda e: e.activation(out=t2, in_=r, func=AF.Sqrt),
               reads=(f"{name}r", f"{name}sq"), writes=(f"{name}sq",))
        p.emit("vector", lambda e: e.reciprocal(r, t2),
               reads=(f"{name}sq",), writes=(f"{name}r",))
    p.barrier()
    with nc.psum_tensor(f"{name}_pm", [128, T], F32) as mbc, \
         nc.psum_tensor(f"{name}_pr", [128, T], F32) as rbc:
        for tc in range(T // 512):
            sl = slice(tc * 512, (tc + 1) * 512)
            p.emit("tensor", lambda e, sl=sl: e.matmul(
                mbc[:, sl], o1[:, :], st[0:1, 0:T][:, sl], start=True, stop=True),
                reads=(f"{name}m", f"{name}o1"), writes=(f"{name}bc",))
            p.emit("tensor", lambda e, sl=sl: e.matmul(
                rbc[:, sl], o1[:, :], st[0:1, T:2 * T][:, sl], start=True, stop=True),
                reads=(f"{name}r", f"{name}o1"), writes=(f"{name}bc",))
        for k in range(ND):
            tile_cb(k, mbc, rbc, f"{name}bc")
    p.barrier()



def _ld3(e, sb, dram_ap, ncols):
    nt = dram_ap.shape[0] // 128
    return e.dma_start(out=sb[:, 0:nt * ncols].rearrange("p (k f) -> p k f", k=nt),
                       in_=dram_ap.rearrange("(k p) f -> p k f", p=128))

def load_cast(p, dram_ap, sb, rows, cols, tag):
    p.emit("gpsimd", lambda e: _ld3(e, sb, dram_ap, cols), writes=(tag,), dma=True)



def build_proj(dout, emit_xf, with_bias, name):
    p = Prog(name)
    nc = p.nc
    T = TLOC
    x = nc.dram_tensor("x", [D, T], F32, kind="ExternalInput").ap()
    wq = nc.dram_tensor("wq", [D, dout], F32, kind="ExternalInput").ap()
    MO = dout // 128
    if with_bias:
        bvec = nc.dram_tensor("bvec", [128, MO], F32, kind="ExternalInput").ap()
    proj = nc.dram_tensor("proj", [dout, T], BF16, kind="ExternalOutput").ap()
    if emit_xf:
        xf_out = nc.dram_tensor("xf", [D, T], F32, kind="ExternalOutput").ap()
    with ExitStack() as es:
        xsb = es.enter_context(nc.sbuf_tensor("xsb", [128, ND * T], F32))
        xnb = es.enter_context(nc.sbuf_tensor("xnb", [128, ND * T], BF16))
        xnf = es.enter_context(nc.sbuf_tensor("xnf", [128, T], F32))
        wsb = es.enter_context(nc.sbuf_tensor("wsb", [128, ND * dout], BF16))
        ev = es.enter_context(nc.sbuf_tensor("ev", [128, T], BF16))
        if with_bias:
            bsb = es.enter_context(nc.sbuf_tensor("bsb", [128, MO], F32))
            p.emit("sync", lambda e: e.dma_start(out=bsb[:, :], in_=bvec[:, :]),
                   writes=("b",), dma=True)
        p.emit("sync", lambda e: _ld3(e, xsb, x, T), writes=("x",), dma=True)
        load_cast(p, wq, wsb, D, dout, "w")

        def norm_tile(k, mbc, rbc, bctag):
            xa = xsb[:, k * T:(k + 1) * T]
            p.emit("vector", lambda e, xa=xa: e.tensor_sub(xnf[:, :], xa, mbc[:, :]),
                   reads=("x", bctag), writes=("xnf",))
            p.emit("vector", lambda e: e.tensor_mul(xnf[:, :], xnf[:, :], rbc[:, :]),
                   reads=("xnf", bctag), writes=("xnf",))
            p.emit("vector", lambda e, k=k: e.tensor_copy(xnb[:, k * T:(k + 1) * T], xnf[:, :]),
                   reads=("xnf",), writes=("xnb",))
            if emit_xf:
                p.emit("sync", lambda e, k=k: e.dma_start(
                    out=xf_out[k * 128:(k + 1) * 128, :], in_=xnf[:, :]),
                    reads=("xnf",), writes=("xfo",), dma=True)

        ln_feature_major(p, es, xsb, "x", T, norm_tile, name + "ln")

        with nc.psum_tensor("pq", [128, 512], F32) as pq:
            for m in range(MO):
                for tc in range(T // 512):
                    sl = slice(tc * 512, (tc + 1) * 512)
                    for k in range(ND):
                        p.emit("tensor", lambda e, m=m, k=k, sl=sl: e.matmul(
                            pq[:, :], wsb[:, k * dout + m * 128:k * dout + (m + 1) * 128],
                            xnb[:, k * T:(k + 1) * T][:, sl],
                            start=(k == 0), stop=(k == ND - 1)),
                            reads=("xnb", "w"), writes=("pq",))
                    if with_bias:
                        p.emit("scalar", lambda e, sl=sl, m=m: e.activation(
                            out=ev[:, sl], in_=pq[:, :], func=AF.Identity, bias=bsb[:, m:m + 1]),
                            reads=("pq", "b"), writes=("ev",))
                    else:
                        p.emit("scalar", lambda e, sl=sl: e.activation(
                            out=ev[:, sl], in_=pq[:, :], func=AF.Copy),
                            reads=("pq",), writes=("ev",))
                p.emit("sync", lambda e, m=m: e.dma_start(
                    out=proj[m * 128:(m + 1) * 128, :], in_=ev[:, :]),
                    reads=("ev",), writes=("po",), dma=True)
        p.finish(("po", "xfo") if emit_xf else ("po",))
    return p.build()



def build_p2(NQ=None, main=True, parts=4.0, ttvar=0):
    p = Prog("p2")
    nc = p.nc
    S = SEQ
    NQ = NQ or (S // 512)
    q_in = nc.dram_tensor("q", [256, S], BF16, kind="ExternalInput").ap()
    k_in = nc.dram_tensor("k", [256, S], BF16, kind="ExternalInput").ap()
    v_in = nc.dram_tensor("v", [S, 260], BF16, kind="ExternalInput").ap()
    att_out = nc.dram_tensor("att", [256, S], BF16, kind="ExternalOutput").ap()
    cm = p.const(causal_masks_np(), "c_cmask")
    ob = p.const(onesblk65_np(), "c_ob65")
    o128f = p.const(np.ones((1, 128), np.float32), "c_o1x128f")
    o65x128 = p.const(np.ones((65, 128), np.float32), "c_o65x128f")
    sm654 = p.const(selmask65x4_np(), "c_sm654")
    rs64 = p.const(rowsel64_np(), "c_rs64")
    with ExitStack() as es:
        qsb = es.enter_context(nc.sbuf_tensor("qsb", [128, 2 * S], BF16))
        ksb = [es.enter_context(nc.sbuf_tensor(f"ksb{i}", [128, 512], BF16)) for i in range(2)]
        vsb = [es.enter_context(nc.sbuf_tensor(f"vsb{i}", [128, 4 * 260], BF16)) for i in range(2)]
        msb = es.enter_context(nc.sbuf_tensor("msb", [128, 4 * 512], F32))
        o65 = es.enter_context(nc.sbuf_tensor("o65", [128, 65], F32))
        of1 = es.enter_context(nc.sbuf_tensor("of1", [1, 128], F32))
        o65w = es.enter_context(nc.sbuf_tensor("o65w", [65, 128], F32))
        sm65 = es.enter_context(nc.sbuf_tensor("sm65", [65, 4], F32))
        cc4 = es.enter_context(nc.sbuf_tensor("cc4", [65, 4], F32))
        rss = es.enter_context(nc.sbuf_tensor("rss", [65, 64], BF16))
        sqq = es.enter_context(nc.sbuf_tensor("sqq", [128, 512], F32))
        kn = es.enter_context(nc.sbuf_tensor("kn", [65, 40], F32))
        qn = es.enter_context(nc.sbuf_tensor("qn", [65, 40], F32))
        cc = es.enter_context(nc.sbuf_tensor("ccb", [65, 4], F32))
        cbias = es.enter_context(nc.sbuf_tensor("cbias", [128, 4], F32))
        prb = es.enter_context(nc.sbuf_tensor("prb", [128, 2 * 512], BF16))
        asb = es.enter_context(nc.sbuf_tensor("asb", [65, 512], F32))
        rb65 = es.enter_context(nc.sbuf_tensor("rb65", [65, 512], BF16))
        rf65 = es.enter_context(nc.sbuf_tensor("rf65", [65, 512], F32))
        lbs = es.enter_context(nc.sbuf_tensor("lbs", [64, 512], F32))
        oat = es.enter_context(nc.sbuf_tensor("oat", [64, 512], F32))

        p.emit("sync", lambda e: e.dma_start(out=qsb[:, 0:S], in_=q_in[0:128, :]), writes=("q0",), dma=True)
        p.emit("sync", lambda e: e.dma_start(out=qsb[:, S:2 * S], in_=q_in[128:256, :]), writes=("q1",), dma=True)
        p.emit("sync", lambda e: e.dma_start(out=msb[:, :], in_=cm.ap()[:, :]), writes=("msk",), dma=True)
        p.emit("sync", lambda e: e.dma_start(out=o65[:, :], in_=ob.ap()[:, :]), writes=("o65",), dma=True)
        p.emit("sync", lambda e: e.dma_start(out=of1[:, :], in_=o128f.ap()[:, :]), writes=("of1",), dma=True)
        p.emit("sync", lambda e: e.dma_start(out=o65w[:, :], in_=o65x128.ap()[:, :]), writes=("o65w",), dma=True)
        p.emit("sync", lambda e: e.dma_start(out=sm65[:, :], in_=sm654.ap()[:, :]), writes=("sm65",), dma=True)
        p.emit("sync", lambda e: e.dma_start(out=rss[:, :], in_=rs64.ap()[:, :]), writes=("rss",), dma=True)
        p.emit("vector", lambda e: e.memset(rb65[:, :], 0.0), writes=("rb",))
        if ttvar == 7:
            p.emit("vector", lambda e: e.memset(lbs[:, :], 1.0), writes=("lbs",))
            p.emit("vector", lambda e: e.memset(asb[:, :], 1.0), writes=("asb",))
            p.emit("vector", lambda e: e.tensor_mul(oat[:, :], asb[0:64, :], lbs[:, :]),
                   reads=("asb", "lbs"), writes=("oat",))

        with nc.psum_tensor("pn", [65, 512], F32) as pn:
            for pr in range(2):
                for sb_ in range(16):
                    sl = slice(sb_ * 512, (sb_ + 1) * 512)
                    p.emit("sync", lambda e, pr=pr, sl=sl: e.dma_start(
                        out=ksb[0][:, :], in_=k_in[pr * 128:(pr + 1) * 128, sl]),
                        writes=("k0",), dma=True)
                    p.emit("scalar", lambda e: e.activation(out=sqq[:, :], in_=ksb[0][:, :], func=AF.Square),
                           reads=("k0",), writes=("sqq",))
                    p.emit("tensor", lambda e: e.matmul(pn[:, :], o65[:, :], sqq[:, :], start=True, stop=True),
                           reads=("sqq", "o65"), writes=("pn",))
                    p.emit("vector", lambda e, pr=pr, sb_=sb_: e.tensor_reduce(
                        kn[:, pr * 16 + sb_:pr * 16 + sb_ + 1], pn[:, :], AX.X, ALU.max),
                        reads=("pn",), writes=("kn",))
                for s in range(NQ):
                    sl = slice(s * 512, (s + 1) * 512)
                    p.emit("scalar", lambda e, pr=pr, sl=sl: e.activation(
                        out=sqq[:, :], in_=qsb[:, pr * S:(pr + 1) * S][:, sl], func=AF.Square),
                        reads=(f"q{pr}",), writes=("sqq",))
                    p.emit("tensor", lambda e: e.matmul(pn[:, :], o65[:, :], sqq[:, :], start=True, stop=True),
                           reads=("sqq", "o65"), writes=("pn",))
                    p.emit("vector", lambda e, pr=pr, s=s: e.tensor_reduce(
                        qn[:, pr * 16 + s:pr * 16 + s + 1], pn[:, :], AX.X, ALU.max),
                        reads=("pn",), writes=("qn",))
        p.barrier()
        for pr in range(2):
            p.emit("vector", lambda e, pr=pr: e.tensor_reduce(
                kn[:, 32 + pr:33 + pr], kn[0:65, pr * 16:(pr + 1) * 16], AX.X, ALU.max),
                reads=("kn",), writes=("knm",))
            p.emit("vector", lambda e, pr=pr: e.tensor_reduce(
                qn[:, 32 + pr:33 + pr], qn[0:65, pr * 16:pr * 16 + NQ], AX.X, ALU.max),
                reads=("qn",), writes=("qnm",))
            p.emit("vector", lambda e, pr=pr: e.tensor_mul(
                cc[:, pr:pr + 1], qn[:, 32 + pr:33 + pr], kn[:, 32 + pr:33 + pr]),
                reads=("qnm", "knm"), writes=("cc",))
        p.emit("vector", lambda e: e.tensor_scalar(cc[:, 0:2], cc[:, 0:2], 0.0, None, ALU.max),
               reads=("cc",), writes=("cc",))
        p.emit("scalar", lambda e: e.activation(out=cc[:, 2:4], in_=cc[:, 0:2], func=AF.Sqrt),
               reads=("cc",), writes=("cc2",))
        p.emit("vector", lambda e: e.tensor_scalar(cc[:, 0:2], cc[:, 2:4], -0.125, None, ALU.mult),
               reads=("cc2",), writes=("cc3",))
        for h in range(4):
            p.emit("vector", lambda e, h=h: e.tensor_mul(
                cc4[:, h:h + 1], cc[:, h // 2:h // 2 + 1], sm65[:, h:h + 1]),
                reads=("cc3", "sm65"), writes=("cc4",))
        with nc.psum_tensor("pcb", [128, 4], F32) as pcb:
            p.emit("tensor", lambda e: e.matmul(
                pcb[:, :], o65w[:, :], cc4[:, :], start=True, stop=True),
                reads=("cc4", "o65w"), writes=("pcb",))
            p.emit("vector", lambda e: e.tensor_copy(cbias[:, :], pcb[:, :]),
                   reads=("pcb",), writes=("cbias",))
        p.barrier()

        if not main:
            return p.build()
        with nc.psum_tensor("ps0", [128, 512], F32) as ps0, \
             nc.psum_tensor("ps1", [128, 512], F32) as ps1, \
             nc.psum_tensor("pa0", [65, 512], F32) as pa0, \
             nc.psum_tensor("pa1", [65, 512], F32) as pa1, \
             nc.psum_tensor("plb", [64, 512], F32) as plb:
            spsum = (ps0, ps1)
            apsum = (pa0, pa1)
            for pr in range(2):
                for s in range(NQ):
                    qsl = qsb[:, pr * S + s * 512:pr * S + (s + 1) * 512]
                    for w in range(s + 1):
                        kb = w % 2
                        p.emit("sync", lambda e, pr=pr, w=w, kb=kb: e.dma_start(
                            out=ksb[kb][:, :], in_=k_in[pr * 128:(pr + 1) * 128, w * 512:(w + 1) * 512]),
                            writes=(f"k{kb}",), dma=True)
                        p.emit("scalar", lambda e, w=w, kb=kb: e.dma_start(
                            out=vsb[kb][:, :].rearrange("p (a c) -> p a c", a=4),
                            in_=v_in[w * 512:(w + 1) * 512, :].rearrange("(a p) c -> p a c", p=128)),
                            writes=(f"v{kb}",), dma=True)
                        for kt in range(4):
                            kvt = 4 * w + kt
                            for hh in range(2):
                                head = 2 * pr + hh
                                ps = spsum[hh]
                                pa = apsum[hh]
                                p.emit("tensor", lambda e, hh=hh, kb=kb, kt=kt, ps=ps, qsl=qsl: e.matmul(
                                    ps[:, :], ksb[kb][64 * hh:64 * (hh + 1), kt * 128:(kt + 1) * 128],
                                    qsl[64 * hh:64 * (hh + 1), :],
                                    start=True, stop=True, tile_position=(64 * hh, 0)),
                                    reads=(f"k{kb}", f"q{pr}"), writes=(f"ps{hh}",))
                                if parts >= 2 and ttvar != 6 and kvt >= 4 * s:
                                    mi = kvt - 4 * s
                                    p.emit("vector", lambda e, hh=hh, mi=mi, ps=ps: e.tensor_add(
                                        ps[:, :], ps[:, :], msb[:, mi * 512:(mi + 1) * 512]),
                                        reads=(f"ps{hh}", "msk"), writes=(f"ps{hh}",))
                                p.emit("scalar", lambda e, hh=hh, ps=ps, head=head: e.activation(
                                    out=prb[:, hh * 512:(hh + 1) * 512], in_=ps[:, :], func=AF.Exp,
                                    bias=cbias[:, head:head + 1], scale=0.125),
                                    reads=(f"ps{hh}", "cbias"), writes=(f"pr{hh}",))
                                if parts < 3:
                                    continue
                                p.emit("tensor", lambda e, hh=hh, kb=kb, kt=kt, pa=pa, kvt=kvt, s=s, head=head: e.matmul(
                                    pa[:, :], vsb[kb][:, kt * 260 + head * 65:kt * 260 + head * 65 + 65],
                                    prb[:, hh * 512:(hh + 1) * 512],
                                    start=(kvt == 0), stop=(kvt == 4 * s + 3)),
                                    reads=(f"pr{hh}", f"v{kb}"), writes=(f"pa{hh}",))
                    for hh in range(2):
                        pa = apsum[hh]
                        if parts >= 3.1:
                            p.emit("scalar", lambda e, pa=pa: e.activation(
                                out=asb[:, :], in_=pa[:, :], func=AF.Copy),
                                reads=(f"pa{hh}",), writes=("asb",))
                        if parts >= 3.2:
                            p.emit("vector", lambda e, pa=pa: e.tensor_scalar(
                                rf65[64:65, :], pa[64:65, :], 1e-30, None, ALU.add),
                                reads=(f"pa{hh}",), writes=("rf",))
                            p.emit("vector", lambda e: e.reciprocal(rf65[64:65, :], rf65[64:65, :]),
                                   reads=("rf",), writes=("rf",))
                            p.emit("vector", lambda e: e.tensor_copy(rb65[64:65, :], rf65[64:65, :]),
                                   reads=("rf",), writes=("rb",))
                        if parts >= 3.3:
                            p.emit("tensor", lambda e: e.matmul(
                                plb[:, :], rss[:, :], rb65[:, :], start=True, stop=True),
                                reads=("rb", "rss"), writes=("plb",))
                            p.emit("scalar", lambda e: e.activation(out=lbs[:, :], in_=plb[:, :], func=AF.Copy),
                                   reads=("plb",), writes=("lbs",))
                        if parts >= 3.4:
                            if ttvar == 1:
                                p.emit("vector", lambda e: e.tensor_mul(oat[:, :], msb[0:64, 0:512], msb[0:64, 0:512]),
                                       reads=("msk",), writes=("oat",))
                            elif ttvar == 2:
                                p.emit("vector", lambda e, hh=hh: e.tensor_mul(oat[:, :], prb[0:64, 0:512], prb[0:64, 0:512]),
                                       reads=(f"pr{hh}",), writes=("oat",))
                            elif ttvar == 3:
                                if pr == 0 and s == 0 and hh == 0:
                                    p.emit("vector", lambda e: e.tensor_mul(oat[:, :], asb[0:64, :], lbs[:, :]),
                                           reads=("asb", "lbs"), writes=("oat",))
                            elif ttvar == 4:
                                p.emit("vector", lambda e: e.tensor_copy(oat[:, :], asb[0:64, :]),
                                       reads=("asb",), writes=("oat",))
                            elif ttvar == 9:
                                p.emit("gpsimd", lambda e: e.tensor_mul(oat[:, :], asb[0:64, :], lbs[:, :]),
                                       reads=("asb", "lbs"), writes=("oat",))
                            elif ttvar == 6:
                                p.emit("vector", lambda e: e.tensor_mul(oat[:, :], asb[0:64, :], lbs[:, :]),
                                       reads=("asb", "lbs"), writes=("oat",))
                            elif ttvar == 5:
                                p.emit("vector", lambda e: e.scalar_tensor_tensor(
                                    oat[:, :], asb[0:64, :], 1.0, lbs[:, :], ALU.mult, ALU.mult),
                                    reads=("asb", "lbs"), writes=("oat",))
                            else:
                                p.emit("gpsimd", lambda e: e.tensor_mul(oat[:, :], asb[0:64, :], lbs[:, :]),
                                       reads=("asb", "lbs"), writes=("oat",))
                        if parts >= 4:
                            p.emit("gpsimd", lambda e, pr=pr, hh=hh, s=s: e.dma_start(
                                out=att_out[(2 * pr + hh) * 64:(2 * pr + hh) * 64 + 64, s * 512:(s + 1) * 512],
                                in_=oat[:, :]),
                                reads=("oat",), writes=("ao",), dma=True)
                        else:
                            p.emit("sync", lambda e, pr=pr, hh=hh, s=s: e.dma_start(
                                out=att_out[(2 * pr + hh) * 64:(2 * pr + hh) * 64 + 64, s * 512:(s + 1) * 512],
                                in_=prb[0:64, hh * 512:(hh + 1) * 512]),
                                reads=(f"pr{hh}",), writes=("ao",), dma=True)
        p.finish(("ao",))
    return p.build()



def build_moe_tail():
    p = Prog("p3")
    nc = p.nc
    T = TLOC
    FF = 4096
    tn = {}
    tn["xres"] = nc.dram_tensor("xres", [D, T], F32, kind="ExternalInput").ap()
    tn["xf"] = nc.dram_tensor("xf", [D, T], F32, kind="ExternalInput").ap()
    tn["att"] = nc.dram_tensor("att", [D, T], BF16, kind="ExternalInput").ap()
    tn["wo"] = nc.dram_tensor("wo", [D, D], F32, kind="ExternalInput").ap()
    tn["gb"] = nc.dram_tensor("gb", [128, ND * 4], F32, kind="ExternalInput").ap()
    tn["wg"] = nc.dram_tensor("wg", [D, 8], F32, kind="ExternalInput").ap()
    tn["w1"] = nc.dram_tensor("w1", [8 * D, FF], F32, kind="ExternalInput").ap()
    tn["bb1"] = nc.dram_tensor("bb1", [8 * 128, FF // 128], F32, kind="ExternalInput").ap()
    tn["w2"] = nc.dram_tensor("w2", [8 * FF, D], F32, kind="ExternalInput").ap()
    tn["bb2"] = nc.dram_tensor("bb2", [8, D], F32, kind="ExternalInput").ap()
    tn["xout"] = nc.dram_tensor("xout", [D, T], F32, kind="ExternalOutput").ap()
    tn["hdram"] = nc.dram_tensor("hdram", [D, T], F32).ap()
    tn["moed"] = nc.dram_tensor("moed", [D, T], F32).ap()
    tn["idf"] = p.const(np.eye(128, dtype=np.float32), "c_idf32")
    tn["oh8"] = p.const(onehot8_np(), "c_onehot8")

    with ExitStack() as es:
        gbs = es.enter_context(nc.sbuf_tensor("gbs", [128, ND * 4 + 16], F32))
        wgs = es.enter_context(nc.sbuf_tensor("wgs", [128, ND * 8], BF16))
        oh8s = es.enter_context(nc.sbuf_tensor("oh8s", [8, 8 * 128], F32))
        wfm = es.enter_context(nc.sbuf_tensor("wfm", [8, T], F32))
        hnb = es.enter_context(nc.sbuf_tensor("hnb", [128, ND * T], BF16))
        p.emit("sync", lambda e: e.dma_start(out=gbs[:, 0:ND * 4], in_=tn["gb"][:, :]),
               writes=("gb",), dma=True)
        load_cast(p, tn["wg"], wgs, D, 8, "wg")
        p.emit("sync", lambda e: e.dma_start(out=oh8s[:, :], in_=tn["oh8"].ap()[:, :]),
               writes=("oh8",), dma=True)

        with ExitStack() as es1:
            atb = es1.enter_context(nc.sbuf_tensor("atb", [128, ND * T], BF16))
            wos = es1.enter_context(nc.sbuf_tensor("wos", [128, ND * D], BF16))
            xft = es1.enter_context(nc.sbuf_tensor("xft", [128, T], F32))
            hsum = es1.enter_context(nc.sbuf_tensor("hsum", [128, ND * T], F32))
            hnf = es1.enter_context(nc.sbuf_tensor("hnf", [128, T], F32))
            ids = es1.enter_context(nc.sbuf_tensor("ids", [128, 128], F32))
            wtok = es1.enter_context(nc.sbuf_tensor("wtok", [128, 16 * 8], F32))
            t16 = es1.enter_context(nc.sbuf_tensor("t16", [128, 16], F32))
            p.emit("sync", lambda e: e.dma_start(out=ids[:, :], in_=tn["idf"].ap()[:, :]),
                   writes=("id",), dma=True)
            p.emit("sync", lambda e: _ld3(e, atb, tn["att"], T), writes=("att",), dma=True)
            load_cast(p, tn["wo"], wos, D, D, "wo")
            with nc.psum_tensor("po", [128, 512], F32) as po:
                for m in range(ND):
                    p.emit("sync", lambda e, m=m: e.dma_start(
                        out=xft[:, :], in_=tn["xf"][m * 128:(m + 1) * 128, :]),
                        writes=("xft",), dma=True)
                    for tc in range(T // 512):
                        sl = slice(tc * 512, (tc + 1) * 512)
                        for k in range(ND):
                            p.emit("tensor", lambda e, m=m, k=k, sl=sl: e.matmul(
                                po[:, :], wos[:, k * D + m * 128:k * D + (m + 1) * 128],
                                atb[:, k * T:(k + 1) * T][:, sl],
                                start=(k == 0), stop=(k == ND - 1)),
                                reads=("att", "wo"), writes=("po",))
                        p.emit("vector", lambda e, m=m, sl=sl: e.tensor_add(
                            hsum[:, m * T:(m + 1) * T][:, sl], po[:, :], xft[:, sl]),
                            reads=("po", "xft"), writes=("hs",))
            p.barrier()

            def h_tile(k, mbc, rbc, bctag):
                ha = hsum[:, k * T:(k + 1) * T]
                p.emit("vector", lambda e, ha=ha: e.tensor_sub(hnf[:, :], ha, mbc[:, :]),
                       reads=("hs", bctag), writes=("hnf",))
                p.emit("vector", lambda e: e.tensor_mul(hnf[:, :], hnf[:, :], rbc[:, :]),
                       reads=("hnf", bctag), writes=("hnf",))
                p.emit("scalar", lambda e, k=k: e.activation(
                    out=hnf[:, :], in_=hnf[:, :], func=AF.Identity,
                    bias=gbs[:, k * 4 + 1:k * 4 + 2], scale=gbs[:, k * 4 + 0:k * 4 + 1]),
                    reads=("hnf", "gb"), writes=("hnf",))
                p.emit("scalar", lambda e, k=k: e.activation(
                    out=hnb[:, k * T:(k + 1) * T], in_=hnf[:, :], func=AF.Copy),
                    reads=("hnf",), writes=("hnb",))
                p.emit("sync", lambda e, k=k: e.dma_start(
                    out=tn["hdram"][k * 128:(k + 1) * 128, :], in_=hnf[:, :]),
                    reads=("hnf",), writes=("hd",), dma=True)

            ln_feature_major(p, es1, hsum, "hs", T, h_tile, "hln")

            with nc.psum_tensor("plg", [128, 8], F32) as plg, \
                 nc.psum_tensor("ptr", [8, 128], F32) as ptr:
                for tt in range(16):
                    for k in range(ND):
                        p.emit("tensor", lambda e, tt=tt, k=k: e.matmul(
                            plg[:, :], hnb[:, k * T + tt * 128:k * T + (tt + 1) * 128],
                            wgs[:, k * 8:(k + 1) * 8],
                            start=(k == 0), stop=(k == ND - 1)),
                            reads=("hnb", "wg"), writes=("plg",))
                    w8 = wtok[:, tt * 8:tt * 8 + 8]
                    p.emit("vector", lambda e, w8=w8: e.tensor_copy(w8, plg[:, :]),
                           reads=("plg",), writes=("wt",))
                    _top2(p, w8, t16, "wt", "t16")
                    p.emit("tensor", lambda e, w8=w8: e.matmul(
                        ptr[:, :], w8, ids[:, :], start=True, stop=True),
                        reads=("wt", "id"), writes=("ptr",))
                    p.emit("vector", lambda e, tt=tt: e.tensor_copy(
                        wfm[:, tt * 128:(tt + 1) * 128], ptr[:, :]),
                        reads=("ptr",), writes=("wfm",))
            p.barrier()

        p.barrier()
        with ExitStack() as es2:
            _dense_experts(p, nc, es2, T, FF, hnb, "hnb", wfm, "wfm", oh8s, "oh8", tn,
                           gelu=AF.Gelu_apprx_tanh)
        p.barrier()

        with ExitStack() as es3:
            hs2 = es3.enter_context(nc.sbuf_tensor("hs2", [128, ND * T], F32))
            ot = es3.enter_context(nc.sbuf_tensor("ot", [128, T], F32))
            bb2s = es3.enter_context(nc.sbuf_tensor("bb2s", [8, D], F32))
            p.emit("sync", lambda e: e.dma_start(out=bb2s[:, :], in_=tn["bb2"][:, :]),
                   writes=("bb2",), dma=True)
            with nc.psum_tensor("pw3", [128, 512], F32) as pw3:
                for dt in range(ND):
                    for tc in range(T // 512):
                        sl = slice(tc * 512, (tc + 1) * 512)
                        p.emit("tensor", lambda e, dt=dt, sl=sl: e.matmul(
                            pw3[:, :], bb2s[:, dt * 128:(dt + 1) * 128], wfm[:, sl],
                            start=True, stop=True),
                            reads=("bb2", "wfm"), writes=("pw3",))
                        p.emit("vector", lambda e, sl=sl: e.tensor_copy(ot[:, 0:512], pw3[:, :]),
                               reads=("pw3",), writes=("ot",))
                        p.emit("gpsimd", lambda e, dt=dt, sl=sl: e.dma_start(
                            out=tn["moed"][dt * 128:(dt + 1) * 128, sl],
                            in_=ot[:, 0:512], accum_op=ALU.add),
                            reads=("ot",), writes=("moeacc",), dma=True)
            p.barrier()
            for k in range(ND):
                p.emit("sync", lambda e, k=k: e.dma_start(
                    out=hs2[:, k * T:(k + 1) * T], in_=tn["hdram"][k * 128:(k + 1) * 128, :]),
                    reads=("hd",), writes=("hs",), dma=True)
                p.emit("gpsimd", lambda e, k=k: e.dma_start(
                    out=hs2[:, k * T:(k + 1) * T], in_=tn["moed"][k * 128:(k + 1) * 128, :],
                    accum_op=ALU.add),
                    reads=("moeacc",), writes=("hs",), dma=True)
            p.emit("vector", lambda e: e.tensor_scalar(
                gbs[:, 32:48:2], gbs[:, 2:34:4], 0.5, None, ALU.mult),
                reads=("gb",), writes=("gb2",))
            p.emit("vector", lambda e: e.tensor_scalar(
                gbs[:, 33:48:2], gbs[:, 3:35:4], 0.5, None, ALU.mult),
                reads=("gb",), writes=("gb2",))

            def out_tile(k, mbc, rbc, bctag):
                ha = hs2[:, k * T:(k + 1) * T]
                p.emit("vector", lambda e, ha=ha: e.tensor_sub(ot[:, :], ha, mbc[:, :]),
                       reads=("hs", bctag), writes=("ot",))
                p.emit("vector", lambda e: e.tensor_mul(ot[:, :], ot[:, :], rbc[:, :]),
                       reads=("ot", bctag), writes=("ot",))
                p.emit("scalar", lambda e, k=k: e.activation(
                    out=ot[:, :], in_=ot[:, :], func=AF.Identity,
                    bias=gbs[:, 33 + 2 * k:34 + 2 * k], scale=gbs[:, 32 + 2 * k:33 + 2 * k]),
                    reads=("ot", "gb2"), writes=("ot",))
                p.emit("gpsimd", lambda e, k=k: e.dma_start(
                    out=ot[:, :], in_=tn["xres"][k * 128:(k + 1) * 128, :],
                    accum_op=ALU.add),
                    reads=("ot",), writes=("ot",), dma=True)
                p.emit("sync", lambda e, k=k: e.dma_start(
                    out=tn["xout"][k * 128:(k + 1) * 128, :], in_=ot[:, :]),
                    reads=("ot",), writes=("xro",), dma=True)

            ln_feature_major(p, es3, hs2, "hs", T, out_tile, "oln")
            p.finish(("xro",))
    return p.build()


def _top2(p, w8, t16, wtag, ttag):
    p.emit("vector", lambda e: e.tensor_reduce(t16[:, 0:1], w8, AX.X, ALU.max, None, None, True),
           reads=(wtag,), writes=(ttag,))
    p.emit("scalar", lambda e: e.activation(out=w8, in_=w8, func=AF.Exp, bias=t16[:, 0:1]),
           reads=(wtag, ttag), writes=(wtag,))
    p.emit("vector", lambda e: e.tensor_reduce(t16[:, 0:1], w8, AX.X, ALU.add),
           reads=(wtag,), writes=(ttag,))
    p.emit("vector", lambda e: e.reciprocal(t16[:, 0:1], t16[:, 0:1]),
           reads=(ttag,), writes=(ttag,))
    p.emit("vector", lambda e: e.tensor_scalar(w8, w8, t16[:, 0:1], None, ALU.mult),
           reads=(wtag, ttag), writes=(wtag,))
    p.emit("vector", lambda e: e.tensor_reduce(t16[:, 1:2], w8, AX.X, ALU.max),
           reads=(wtag,), writes=(ttag,))
    p.emit("vector", lambda e: e.tensor_scalar(t16[:, 8:16], w8, t16[:, 1:2], -1e30, ALU.is_ge, ALU.mult),
           reads=(wtag, ttag), writes=(ttag,))
    p.emit("vector", lambda e: e.tensor_add(t16[:, 8:16], w8, t16[:, 8:16]),
           reads=(wtag, ttag), writes=(ttag,))
    p.emit("vector", lambda e: e.tensor_reduce(t16[:, 2:3], t16[:, 8:16], AX.X, ALU.max),
           reads=(ttag,), writes=(ttag,))
    p.emit("vector", lambda e: e.tensor_scalar(t16[:, 8:16], w8, t16[:, 2:3], None, ALU.is_ge),
           reads=(wtag, ttag), writes=(ttag,))
    p.emit("vector", lambda e: e.tensor_add(t16[:, 3:4], t16[:, 1:2], t16[:, 2:3]),
           reads=(ttag,), writes=(ttag,))
    p.emit("vector", lambda e: e.tensor_scalar(t16[:, 3:4], t16[:, 3:4], 1e-9, None, ALU.max),
           reads=(ttag,), writes=(ttag,))
    p.emit("vector", lambda e: e.reciprocal(t16[:, 4:5], t16[:, 3:4]),
           reads=(ttag,), writes=(ttag,))
    p.emit("vector", lambda e: e.tensor_mul(w8, w8, t16[:, 8:16]),
           reads=(wtag, ttag), writes=(wtag,))
    p.emit("vector", lambda e: e.tensor_scalar(w8, w8, t16[:, 4:5], None, ALU.mult),
           reads=(wtag, ttag), writes=(wtag,))


def _dense_experts(p, nc, es, T, FF, hnb, htag, wfm, wtag, oh8s, ohtag, tn, gelu):
    NT = T // 256
    NFS = FF // 512
    NKT = FF // 128
    w1s = es.enter_context(nc.sbuf_tensor("w1s", [128, ND * FF], BF16))
    w2s = es.enter_context(nc.sbuf_tensor("w2s", [128, NKT * D], BF16))
    b1s = es.enter_context(nc.sbuf_tensor("b1s", [128, NKT], F32))
    h1f = es.enter_context(nc.sbuf_tensor("h1f", [128, NKT * 256], BF16))
    wbc = es.enter_context(nc.sbuf_tensor("wbc", [128, T], F32))
    ytmp = es.enter_context(nc.sbuf_tensor("ytmp", [128, ND * 256], F32))

    with nc.psum_tensor("ph0", [128, 512], F32) as ph0, \
         nc.psum_tensor("ph1", [128, 512], F32) as ph1, \
         nc.psum_tensor("py0", [128, 256], F32) as py0, \
         nc.psum_tensor("py1", [128, 256], F32) as py1, \
         nc.psum_tensor("pw", [128, 512], F32) as pw:
        phs = (ph0, ph1)
        pys = (py0, py1)
        for e_ in range(8):
            p.emit("gpsimd", lambda e, e_=e_: _ld3(e, w1s, tn["w1"][e_ * D:(e_ + 1) * D, :], FF),
                   writes=("w1",), dma=True)
            p.emit("gpsimd", lambda e, e_=e_: _ld3(e, w2s, tn["w2"][e_ * FF:(e_ + 1) * FF, :], D),
                   writes=("w2",), dma=True)
            p.emit("sync", lambda e, e_=e_: e.dma_start(
                out=b1s[:, :], in_=tn["bb1"][e_ * 128:(e_ + 1) * 128, :]),
                writes=("b1",), dma=True)
            for tc in range(T // 512):
                sl = slice(tc * 512, (tc + 1) * 512)
                p.emit("tensor", lambda e, e_=e_, sl=sl: e.matmul(
                    pw[:, :], oh8s[:, e_ * 128:(e_ + 1) * 128], wfm[:, sl],
                    start=True, stop=True),
                    reads=(wtag, ohtag), writes=("pw",))
                p.emit("scalar", lambda e, sl=sl: e.activation(
                    out=wbc[:, sl], in_=pw[:, :], func=AF.Copy),
                    reads=("pw",), writes=("wbc",))
            for tt in range(NT):
                tsl = slice(tt * 256, (tt + 1) * 256)
                for fs in range(NFS):
                    for fsub in range(4):
                        kt = fs * 4 + fsub
                        ph = phs[fsub // 2]
                        pc = slice((fsub % 2) * 256, (fsub % 2) * 256 + 256)
                        for k in range(ND):
                            p.emit("tensor", lambda e, fs=fs, fsub=fsub, k=k, ph=ph, pc=pc, tsl=tsl: e.matmul(
                                ph[:, pc], w1s[:, k * FF + fs * 512 + fsub * 128:k * FF + fs * 512 + (fsub + 1) * 128],
                                hnb[:, k * T:(k + 1) * T][:, tsl],
                                start=(k == 0), stop=(k == ND - 1)),
                                reads=(htag, "w1"), writes=(f"ph{fsub // 2}",))
                        p.emit("scalar", lambda e, kt=kt, ph=ph, pc=pc: e.activation(
                            out=h1f[:, kt * 256:(kt + 1) * 256], in_=ph[:, pc],
                            func=gelu, bias=b1s[:, kt:kt + 1]),
                            reads=(f"ph{fsub // 2}", "b1"), writes=("h1f",))
                for dt in range(ND):
                    py = pys[dt % 2]
                    for kt in range(NKT):
                        p.emit("tensor", lambda e, kt=kt, dt=dt, py=py: e.matmul(
                            py[:, :], w2s[:, kt * D + dt * 128:kt * D + (dt + 1) * 128],
                            h1f[:, kt * 256:(kt + 1) * 256],
                            start=(kt == 0), stop=(kt == NKT - 1)),
                            reads=("h1f", "w2"), writes=(f"py{dt % 2}",))
                    p.emit("vector", lambda e, dt=dt, py=py, tsl=tsl: e.tensor_mul(
                        ytmp[:, dt * 256:(dt + 1) * 256], py[:, :], wbc[:, tsl]),
                        reads=(f"py{dt % 2}", "wbc"), writes=("ytmp",))
                acc = (ALU.add if e_ > 0 else ALU.bypass)
                p.emit("gpsimd", lambda e, tt=tt, acc=acc: e.dma_start(
                    out=tn["moed"].rearrange("(k p) t -> p k t", p=128)[:, :, tt * 256:(tt + 1) * 256],
                    in_=ytmp[:, :].rearrange("p (k t) -> p k t", k=ND),
                    accum_op=acc),
                    reads=("ytmp",), writes=("moeacc",), dma=True)
    p.barrier()



def build_proj_residual(with_bias, scale, name):
    p = Prog(name)
    nc = p.nc
    T = TLOC
    xres = nc.dram_tensor("xres", [D, T], F32, kind="ExternalInput").ap()
    srcb = nc.dram_tensor("srcb", [D, T], BF16, kind="ExternalInput").ap()
    w = nc.dram_tensor("w", [D, D], F32, kind="ExternalInput").ap()
    if with_bias:
        bvec = nc.dram_tensor("bvec", [128, ND], F32, kind="ExternalInput").ap()
    xout = nc.dram_tensor("xout", [D, T], F32, kind="ExternalOutput").ap()
    with ExitStack() as es:
        ssb = es.enter_context(nc.sbuf_tensor("ssb", [128, ND * T], BF16))
        wsb = es.enter_context(nc.sbuf_tensor("wsb", [128, ND * D], BF16))
        ot = es.enter_context(nc.sbuf_tensor("ot", [128, T], F32))
        if with_bias:
            bsb = es.enter_context(nc.sbuf_tensor("bsb", [128, ND], F32))
            p.emit("sync", lambda e: e.dma_start(out=bsb[:, :], in_=bvec[:, :]),
                   writes=("b",), dma=True)
        p.emit("sync", lambda e: _ld3(e, ssb, srcb, T), writes=("s",), dma=True)
        load_cast(p, w, wsb, D, D, "w")
        with nc.psum_tensor("pq", [128, 512], F32) as pq:
            for m in range(ND):
                for tc in range(T // 512):
                    sl = slice(tc * 512, (tc + 1) * 512)
                    for k in range(ND):
                        p.emit("tensor", lambda e, m=m, k=k, sl=sl: e.matmul(
                            pq[:, :], wsb[:, k * D + m * 128:k * D + (m + 1) * 128],
                            ssb[:, k * T:(k + 1) * T][:, sl],
                            start=(k == 0), stop=(k == ND - 1)),
                            reads=("s", "w"), writes=("pq",))
                    if with_bias:
                        p.emit("scalar", lambda e, sl=sl, m=m: e.activation(
                            out=ot[:, sl], in_=pq[:, :], func=AF.Identity,
                            bias=bsb[:, m:m + 1], scale=float(scale)),
                            reads=("pq", "b"), writes=("ot",))
                    else:
                        p.emit("scalar", lambda e, sl=sl: e.activation(
                            out=ot[:, sl], in_=pq[:, :], func=AF.Identity,
                            scale=float(scale)),
                            reads=("pq",), writes=("ot",))
                p.emit("gpsimd", lambda e, m=m: e.dma_start(
                    out=ot[:, :], in_=xres[m * 128:(m + 1) * 128, :], accum_op=ALU.add),
                    reads=("ot",), writes=("ot",), dma=True)
                p.emit("sync", lambda e, m=m: e.dma_start(
                    out=xout[m * 128:(m + 1) * 128, :], in_=ot[:, :]),
                    reads=("ot",), writes=("xo",), dma=True)
        p.finish(("xo",))
    return p.build()



def build_router():
    p = Prog("p4")
    nc = p.nc
    T = TLOC
    x = nc.dram_tensor("x", [D, T], F32, kind="ExternalInput").ap()
    wr = nc.dram_tensor("wr", [D, 8], F32, kind="ExternalInput").ap()
    xnb_out = nc.dram_tensor("xnb", [D, T], BF16, kind="ExternalOutput").ap()
    wt_out = nc.dram_tensor("wt", [T, 8], F32, kind="ExternalOutput").ap()
    with ExitStack() as es:
        xsb = es.enter_context(nc.sbuf_tensor("xsb", [128, ND * T], F32))
        xnb = es.enter_context(nc.sbuf_tensor("xnbs", [128, ND * T], BF16))
        xnf = es.enter_context(nc.sbuf_tensor("xnf", [128, T], F32))
        wrs = es.enter_context(nc.sbuf_tensor("wrs", [128, ND * 8], BF16))
        wtok = es.enter_context(nc.sbuf_tensor("wtok", [128, 16 * 8], F32))
        t16 = es.enter_context(nc.sbuf_tensor("t16", [128, 16], F32))
        p.emit("sync", lambda e: _ld3(e, xsb, x, T), writes=("x",), dma=True)
        load_cast(p, wr, wrs, D, 8, "wr")

        def norm_tile(k, mbc, rbc, bctag):
            xa = xsb[:, k * T:(k + 1) * T]
            p.emit("vector", lambda e, xa=xa: e.tensor_sub(xnf[:, :], xa, mbc[:, :]),
                   reads=("x", bctag), writes=("xnf",))
            p.emit("vector", lambda e: e.tensor_mul(xnf[:, :], xnf[:, :], rbc[:, :]),
                   reads=("xnf", bctag), writes=("xnf",))
            p.emit("vector", lambda e, k=k: e.tensor_copy(xnb[:, k * T:(k + 1) * T], xnf[:, :]),
                   reads=("xnf",), writes=("xnb",))
            p.emit("sync", lambda e, k=k: e.dma_start(
                out=xnb_out[k * 128:(k + 1) * 128, :], in_=xnb[:, k * T:(k + 1) * T]),
                reads=("xnb",), writes=("xno",), dma=True)

        ln_feature_major(p, es, xsb, "x", T, norm_tile, "rln")
        with nc.psum_tensor("plg", [128, 8], F32) as plg:
            for tt in range(16):
                for k in range(ND):
                    p.emit("tensor", lambda e, tt=tt, k=k: e.matmul(
                        plg[:, :], xnb[:, k * T + tt * 128:k * T + (tt + 1) * 128],
                        wrs[:, k * 8:(k + 1) * 8],
                        start=(k == 0), stop=(k == ND - 1)),
                        reads=("xnb", "wr"), writes=("plg",))
                w8 = wtok[:, tt * 8:tt * 8 + 8]
                p.emit("vector", lambda e, w8=w8: e.tensor_copy(w8, plg[:, :]),
                       reads=("plg",), writes=("wt",))
                _top2(p, w8, t16, "wt", "t16")
        p.barrier()
        p.emit("sync", lambda e: e.dma_start(
            out=wt_out.rearrange("(tt p) e -> p tt e", p=128),
            in_=wtok[:, :].rearrange("p (tt e) -> p tt e", e=8)),
            reads=("wt",), writes=("wto",), dma=True)
        p.finish(("wto", "xno"))
    return p.build()



def build_masks():
    p = Prog("p5")
    nc = p.nc
    TG = 16384
    CAP = 5120
    wt = nc.dram_tensor("wt", [TG, 8], F32, kind="ExternalInput").ap()
    gates_out = nc.dram_tensor("gates", [TG, 8], F32, kind="ExternalOutput").ap()
    fb_out = nc.dram_tensor("fb", [TG, 1], F32, kind="ExternalOutput").ap()
    lts = p.const(np.tril(np.ones((128, 128), np.float32), -1).T, "c_ltstrict")
    with ExitStack() as es:
        t1 = es.enter_context(nc.sbuf_tensor("t1", [128, 128 * 8], F32))
        gt = es.enter_context(nc.sbuf_tensor("gt", [128, 128 * 8], F32))
        fb = es.enter_context(nc.sbuf_tensor("fbs", [128, 128], F32))
        fbi = es.enter_context(nc.sbuf_tensor("fbi", [128, 128], F32))
        el = es.enter_context(nc.sbuf_tensor("el", [128, 128], F32))
        cs = es.enter_context(nc.sbuf_tensor("cs", [128, 128], F32))
        k1 = es.enter_context(nc.sbuf_tensor("k1", [128, 128], F32))
        kp = es.enter_context(nc.sbuf_tensor("kp", [128, 128], F32))
        ofs = es.enter_context(nc.sbuf_tensor("ofs", [128, 1], F32))
        ltsb = es.enter_context(nc.sbuf_tensor("ltsb", [128, 128], F32))
        p.emit("sync", lambda e: e.dma_start(
            out=t1[:, :].rearrange("p (i e) -> p i e", e=8),
            in_=wt.rearrange("(p i) e -> p i e", p=128)),
            writes=("t1",), dma=True)
        p.emit("sync", lambda e: e.dma_start(out=ltsb[:, :], in_=lts.ap()[:, :]),
               writes=("lt",), dma=True)
        p.emit("vector", lambda e: e.memset(fb[:, :], 0.0), writes=("fb",))
        p.emit("vector", lambda e: e.memset(fbi[:, :], 1.0), writes=("fbi",))
        with nc.psum_tensor("pofs", [128, 1], F32) as pofs:
            for e_ in range(8):
                we = t1[:, e_:128 * 8:8]
                p.emit("vector", lambda e, we=we: e.tensor_scalar(
                    el[:, :], we, 0.0, None, ALU.is_gt),
                    reads=("t1",), writes=("el",))
                p.emit("gpsimd", lambda e: e.tensor_mul(el[:, :], el[:, :], fbi[:, :]),
                       reads=("el", "fbi"), writes=("el",))
                p.emit("vector", lambda e: e.tensor_tensor_scan(
                    cs[:, :], el[:, :], el[:, :], 0.0, ALU.add, ALU.bypass),
                    reads=("el",), writes=("cs",))
                p.emit("tensor", lambda e: e.matmul(
                    pofs[:, :], ltsb[:, :], cs[:, 127:128], start=True, stop=True),
                    reads=("cs", "lt"), writes=("pofs",))
                p.emit("vector", lambda e: e.tensor_copy(ofs[:, :], pofs[:, :]),
                       reads=("pofs",), writes=("ofs",))
                p.emit("vector", lambda e: e.tensor_scalar(
                    cs[:, :], cs[:, :], ofs[:, 0:1], None, ALU.add),
                    reads=("cs", "ofs"), writes=("cs",))
                p.emit("vector", lambda e: e.tensor_scalar(
                    k1[:, :], cs[:, :], float(CAP) + 0.5, None, ALU.is_le),
                    reads=("cs",), writes=("k1",))
                p.emit("gpsimd", lambda e: e.tensor_mul(kp[:, :], el[:, :], k1[:, :]),
                       reads=("el", "k1"), writes=("kp",))
                p.emit("gpsimd", lambda e: e.tensor_add(fb[:, :], fb[:, :], el[:, :]),
                       reads=("fb", "el"), writes=("fb",))
                p.emit("gpsimd", lambda e: e.tensor_sub(fb[:, :], fb[:, :], kp[:, :]),
                       reads=("fb", "kp"), writes=("fb",))
                p.emit("vector", lambda e: e.tensor_scalar(
                    fbi[:, :], fb[:, :], -1.0, 1.0, ALU.mult, ALU.add),
                    reads=("fb",), writes=("fbi",))
                p.emit("gpsimd", lambda e, we=we, e_=e_: e.tensor_mul(
                    gt[:, e_:128 * 8:8], kp[:, :], we),
                    reads=("kp", "t1"), writes=("gt",))
        p.emit("sync", lambda e: e.dma_start(
            out=gates_out.rearrange("(p i) e -> p i e", p=128),
            in_=gt[:, :].rearrange("p (i e) -> p i e", e=8)),
            reads=("gt",), writes=("go",), dma=True)
        p.emit("sync", lambda e: e.dma_start(
            out=fb_out.rearrange("(p i) o -> p (i o)", p=128), in_=fb[:, :]),
            reads=("fb",), writes=("fo",), dma=True)
        p.finish(("go", "fo"))
    return p.build()



def build_moe2():
    p = Prog("p6")
    nc = p.nc
    T = TLOC
    FF = 4096
    tn = {}
    tn["xres"] = nc.dram_tensor("xres", [D, T], F32, kind="ExternalInput").ap()
    tn["xnb"] = nc.dram_tensor("xnb", [D, T], BF16, kind="ExternalInput").ap()
    tn["gates"] = nc.dram_tensor("gates", [T, 8], F32, kind="ExternalInput").ap()
    tn["fbv"] = nc.dram_tensor("fbv", [T, 1], F32, kind="ExternalInput").ap()
    tn["fbt"] = nc.dram_tensor("fbt", [128, 16], F32, kind="ExternalInput").ap()
    tn["w1"] = nc.dram_tensor("w1", [8 * D, FF], F32, kind="ExternalInput").ap()
    tn["bb1"] = nc.dram_tensor("bb1", [8 * 128, FF // 128], F32, kind="ExternalInput").ap()
    tn["w2"] = nc.dram_tensor("w2", [8 * FF, D], F32, kind="ExternalInput").ap()
    tn["bb2"] = nc.dram_tensor("bb2", [8, D], F32, kind="ExternalInput").ap()
    tn["fw1"] = nc.dram_tensor("fw1", [D, FF], F32, kind="ExternalInput").ap()
    tn["fb1"] = nc.dram_tensor("fb1", [128, FF // 128], F32, kind="ExternalInput").ap()
    tn["fw2"] = nc.dram_tensor("fw2", [FF, D], F32, kind="ExternalInput").ap()
    tn["fb2"] = nc.dram_tensor("fb2", [1, D], F32, kind="ExternalInput").ap()
    tn["xout"] = nc.dram_tensor("xout", [D, T], F32, kind="ExternalOutput").ap()
    tn["moed"] = nc.dram_tensor("moed", [D, T], F32).ap()
    tn["idf"] = p.const(np.eye(128, dtype=np.float32), "c_idf32")
    tn["oh8"] = p.const(onehot8_np(), "c_onehot8")
    ones1x256 = p.const(np.ones((1, 256), np.float32), "c_ones1x256")

    with ExitStack() as es:
        oh8s = es.enter_context(nc.sbuf_tensor("oh8s", [8, 8 * 128], F32))
        wfm = es.enter_context(nc.sbuf_tensor("wfm", [8, T], F32))
        xnb = es.enter_context(nc.sbuf_tensor("xnbs", [128, ND * T], BF16))
        p.emit("sync", lambda e: e.dma_start(out=oh8s[:, :], in_=tn["oh8"].ap()[:, :]),
               writes=("oh8",), dma=True)
        p.emit("sync", lambda e: _ld3(e, xnb, tn["xnb"], T), writes=("xnb",), dma=True)

        with ExitStack() as es0:
            gts = es0.enter_context(nc.sbuf_tensor("gts", [128, 16 * 8], F32))
            fbt = es0.enter_context(nc.sbuf_tensor("fbts", [128, 16], F32))
            ids = es0.enter_context(nc.sbuf_tensor("ids", [128, 128], F32))
            p.emit("sync", lambda e: e.dma_start(
                out=gts[:, :].rearrange("p (tt e) -> p tt e", e=8),
                in_=tn["gates"].rearrange("(tt p) e -> p tt e", p=128)),
                writes=("gts",), dma=True)
            p.emit("sync", lambda e: e.dma_start(out=fbt[:, :], in_=tn["fbt"][:, :]),
                writes=("fbt",), dma=True)
            p.emit("sync", lambda e: e.dma_start(out=ids[:, :], in_=tn["idf"].ap()[:, :]),
                   writes=("id",), dma=True)
            p.emit("vector", lambda e: e.tensor_scalar(
                fbt[:, :], fbt[:, :], -1.0, 1.0, ALU.mult, ALU.add),
                reads=("fbt",), writes=("fbt",))
            with nc.psum_tensor("ptr", [8, 128], F32) as ptr:
                for tt in range(16):
                    p.emit("vector", lambda e, tt=tt: e.tensor_scalar(
                        gts[:, tt * 8:(tt + 1) * 8], gts[:, tt * 8:(tt + 1) * 8],
                        fbt[:, tt:tt + 1], None, ALU.mult),
                        reads=("gts", "fbt"), writes=("gts",))
                    p.emit("tensor", lambda e, tt=tt: e.matmul(
                        ptr[:, :], gts[:, tt * 8:(tt + 1) * 8], ids[:, :],
                        start=True, stop=True),
                        reads=("gts", "id"), writes=("ptr",))
                    p.emit("vector", lambda e, tt=tt: e.tensor_copy(
                        wfm[:, tt * 128:(tt + 1) * 128], ptr[:, :]),
                        reads=("ptr",), writes=("wfm",))
            p.barrier()
        p.barrier()

        with ExitStack() as es2:
            _dense_experts(p, nc, es2, T, FF, xnb, "xnb", wfm, "wfm", oh8s, "oh8", tn,
                           gelu=AF.Gelu)
        p.barrier()

        with ExitStack() as es3:
            bb2s = es3.enter_context(nc.sbuf_tensor("bb2s", [8, D], F32))
            ot = es3.enter_context(nc.sbuf_tensor("ot", [128, T], F32))
            p.emit("sync", lambda e: e.dma_start(out=bb2s[:, :], in_=tn["bb2"][:, :]),
                   writes=("bb2",), dma=True)
            with nc.psum_tensor("pw3", [128, 512], F32) as pw3:
                for dt in range(ND):
                    for tc in range(T // 512):
                        sl = slice(tc * 512, (tc + 1) * 512)
                        p.emit("tensor", lambda e, dt=dt, sl=sl: e.matmul(
                            pw3[:, :], bb2s[:, dt * 128:(dt + 1) * 128], wfm[:, sl],
                            start=True, stop=True),
                            reads=("bb2", "wfm"), writes=("pw3",))
                        p.emit("vector", lambda e, sl=sl: e.tensor_copy(ot[:, 0:512], pw3[:, :]),
                               reads=("pw3",), writes=("ot",))
                        p.emit("gpsimd", lambda e, dt=dt, sl=sl: e.dma_start(
                            out=tn["moed"][dt * 128:(dt + 1) * 128, sl],
                            in_=ot[:, 0:512], accum_op=ALU.add),
                            reads=("ot",), writes=("moeacc",), dma=True)
            p.barrier()

        with ExitStack() as es4:
            fw1s = es4.enter_context(nc.sbuf_tensor("fw1s", [128, ND * FF], BF16))
            fw2s = es4.enter_context(nc.sbuf_tensor("fw2s", [128, (FF // 128) * D], BF16))
            fb1s = es4.enter_context(nc.sbuf_tensor("fb1s", [128, FF // 128], F32))
            fb2s = es4.enter_context(nc.sbuf_tensor("fb2s", [1, D], F32))
            o256b = es4.enter_context(nc.sbuf_tensor("o256b", [1, 256], F32))
            h1f = es4.enter_context(nc.sbuf_tensor("h1f2", [128, (FF // 128) * 256], BF16))
            fbbc = es4.enter_context(nc.sbuf_tensor("fbbc", [128, T], F32))
            ytmp = es4.enter_context(nc.sbuf_tensor("ytmp2", [128, 256], F32))
            NKT = FF // 128
            p.emit("gpsimd", lambda e: _ld3(e, fw1s, tn["fw1"], FF), writes=("fw1",), dma=True)
            p.emit("gpsimd", lambda e: _ld3(e, fw2s, tn["fw2"], D), writes=("fw2",), dma=True)
            p.emit("sync", lambda e: e.dma_start(out=fb1s[:, :], in_=tn["fb1"][:, :]),
                   writes=("fb1",), dma=True)
            p.emit("sync", lambda e: e.dma_start(out=fb2s[:, :], in_=tn["fb2"][:, :]),
                   writes=("fb2",), dma=True)
            p.emit("sync", lambda e: e.dma_start(out=o256b[:, :], in_=ones1x256.ap()[:, :]),
                   writes=("o256b",), dma=True)
            with nc.psum_tensor("fph0", [128, 512], F32) as ph0, \
                 nc.psum_tensor("fph1", [128, 512], F32) as ph1, \
                 nc.psum_tensor("fpy0", [128, 256], F32) as py0, \
                 nc.psum_tensor("fpy1", [128, 256], F32) as py1, \
                 nc.psum_tensor("pfb", [128, 512], F32) as pfb:
                with ExitStack() as es4a:
                    fbr = es4a.enter_context(nc.sbuf_tensor("fbr", [1, 512], F32))
                    for tc in range(T // 512):
                        sl = slice(tc * 512, (tc + 1) * 512)
                        p.emit("sync", lambda e, sl=sl: e.dma_start(
                            out=fbr[:, :], in_=tn["fbv"].rearrange("t o -> o t")[0:1, sl]),
                            writes=("fbr",), dma=True)
                        p.emit("tensor", lambda e, sl=sl, fbr=fbr: e.matmul(
                            pfb[:, :], o256b[0:1, 0:128], fbr[0:1, :], start=True, stop=True),
                            reads=("fbr", "o256b"), writes=("pfb",))
                        p.emit("scalar", lambda e, sl=sl: e.activation(
                            out=fbbc[:, sl], in_=pfb[:, :], func=AF.Copy),
                            reads=("pfb",), writes=("fbbc",))
                    p.barrier()
                phs = (ph0, ph1)
                pys = (py0, py1)
                for tt in range(T // 256):
                    tsl = slice(tt * 256, (tt + 1) * 256)
                    for fs in range(FF // 512):
                        for fsub in range(4):
                            kt = fs * 4 + fsub
                            ph = phs[fsub // 2]
                            pc = slice((fsub % 2) * 256, (fsub % 2) * 256 + 256)
                            for k in range(ND):
                                p.emit("tensor", lambda e, fs=fs, fsub=fsub, k=k, ph=ph, pc=pc, tsl=tsl: e.matmul(
                                    ph[:, pc], fw1s[:, k * FF + fs * 512 + fsub * 128:k * FF + fs * 512 + (fsub + 1) * 128],
                                    xnb[:, k * T:(k + 1) * T][:, tsl],
                                    start=(k == 0), stop=(k == ND - 1)),
                                    reads=("xnb", "fw1"), writes=(f"fph{fsub // 2}",))
                            p.emit("scalar", lambda e, kt=kt, ph=ph, pc=pc: e.activation(
                                out=h1f[:, kt * 256:(kt + 1) * 256], in_=ph[:, pc],
                                func=AF.Gelu, bias=fb1s[:, kt:kt + 1]),
                                reads=(f"fph{fsub // 2}", "fb1"), writes=("fh1",))
                    for dt in range(ND):
                        py = pys[dt % 2]
                        for kt in range(NKT):
                            p.emit("tensor", lambda e, kt=kt, dt=dt, py=py: e.matmul(
                                py[:, :], fw2s[:, kt * D + dt * 128:kt * D + (dt + 1) * 128],
                                h1f[:, kt * 256:(kt + 1) * 256],
                                start=(kt == 0), stop=False),
                                reads=("fh1", "fw2"), writes=(f"fpy{dt % 2}",))
                        p.emit("tensor", lambda e, dt=dt, py=py: e.matmul(
                            py[:, :], tn_fb2_lhsT(p, fb2s, dt), o256b[0:1, :],
                            start=False, stop=True),
                            reads=("fb2", "o256b"), writes=(f"fpy{dt % 2}",))
                        p.emit("vector", lambda e, dt=dt, py=py, tsl=tsl: e.tensor_mul(
                            ytmp[:, :], py[:, :], fbbc[:, tsl]),
                            reads=(f"fpy{dt % 2}", "fbbc"), writes=("ytm2",))
                        p.emit("gpsimd", lambda e, tt=tt, dt=dt: e.dma_start(
                            out=tn["moed"][dt * 128:(dt + 1) * 128, tt * 256:(tt + 1) * 256],
                            in_=ytmp[:, :], accum_op=ALU.add),
                            reads=("ytm2",), writes=("moeacc",), dma=True)
            p.barrier()

        with ExitStack() as es5:
            ot2 = es5.enter_context(nc.sbuf_tensor("ot2", [128, T], F32))
            for k in range(ND):
                p.emit("sync", lambda e, k=k: e.dma_start(
                    out=ot2[:, :], in_=tn["moed"][k * 128:(k + 1) * 128, :]),
                    reads=("moeacc", "oo"), writes=("ot2",), dma=True)
                p.emit("vector", lambda e: e.tensor_scalar(
                    ot2[:, :], ot2[:, :], 0.5, None, ALU.mult),
                    reads=("ot2",), writes=("ot2",))
                p.emit("gpsimd", lambda e, k=k: e.dma_start(
                    out=ot2[:, :], in_=tn["xres"][k * 128:(k + 1) * 128, :],
                    accum_op=ALU.add),
                    reads=("ot2",), writes=("ot2",), dma=True)
                p.emit("sync", lambda e, k=k: e.dma_start(
                    out=tn["xout"][k * 128:(k + 1) * 128, :], in_=ot2[:, :]),
                    reads=("ot2",), writes=("oo",), dma=True)
            p.finish(("oo",))
    return p.build()


def tn_fb2_lhsT(p, fb2s, dt):
    return fb2s[0:1, dt * 128:(dt + 1) * 128]





def fft_consts_np():
    w = -2j * np.pi
    n2 = np.arange(64)
    k2 = np.arange(128)
    j = np.arange(128)
    k1 = np.arange(128)
    W2f = np.exp(w * (n2[None, :, None] * k2[None, None, :] / 128.0
                      + j[:, None, None] * k2[None, None, :] / 16384.0))
    W1 = np.exp(w * (j[:, None] * k1[None, :]) / 128.0)
    W1i = np.exp(-w * (k1[:, None] * j[None, :]) / 128.0)
    W2i = np.exp(-w * (k2[None, :, None] * n2[None, None, :] / 128.0
                       + j[:, None, None] * k2[None, :, None] / 16384.0)) / 16384.0
    d = {}
    d["fw2f_re"] = np.ascontiguousarray(W2f.real.transpose(1, 0, 2).reshape(64, 128 * 128).astype(BF))
    d["fw2f_im"] = np.ascontiguousarray(W2f.imag.transpose(1, 0, 2).reshape(64, 128 * 128).astype(BF))
    d["w1r"] = np.ascontiguousarray(W1.real.astype(BF))
    d["w1i"] = np.ascontiguousarray(W1.imag.astype(BF))
    d["w1in"] = np.ascontiguousarray((-W1.imag).astype(BF))
    d["v1r"] = np.ascontiguousarray(W1i.real.astype(BF))
    d["v1i"] = np.ascontiguousarray(W1i.imag.astype(BF))
    d["v1in"] = np.ascontiguousarray((-W1i.imag).astype(BF))
    d["fw2i_re"] = np.ascontiguousarray(W2i.real.transpose(1, 0, 2).reshape(128, 128 * 64).astype(BF))
    d["fw2i_imn"] = np.ascontiguousarray((-W2i.imag).transpose(1, 0, 2).reshape(128, 128 * 64).astype(BF))
    return d


def build_p8(nph=7):
    p = Prog("p8")
    nc = p.nc
    S = SEQ
    u1 = nc.dram_tensor("u1", [64, 2 * 16384], BF16, kind="ExternalInput").ap()
    ubd = nc.dram_tensor("ubd", [128, 2 * S], BF16, kind="ExternalInput").ap()
    As = nc.dram_tensor("As", [128, 64], F32, kind="ExternalInput").ap()
    dts = nc.dram_tensor("dts", [128, 1], F32, kind="ExternalInput").ap()
    Bs = nc.dram_tensor("Bs", [128, 64], F32, kind="ExternalInput").ap()
    Cs = nc.dram_tensor("Cs", [128, 64], F32, kind="ExternalInput").ap()
    Dps = nc.dram_tensor("Dps", [128, 1], F32, kind="ExternalInput").ap()
    yout = nc.dram_tensor("yout", [128, 2 * S], BF16, kind="ExternalOutput").ap()
    kdram = nc.dram_tensor("kdram", [128, S], BF16).ap()
    kfdram = nc.dram_tensor("kfdram", [128, 2 * 16384], BF16).ap()
    dpudram = nc.dram_tensor("dpudram", [128, S], F32).ap()
    cn = fft_consts_np()
    fw2f_d = [p.const(cn["fw2f_re"], "c_fw2f_re"), p.const(cn["fw2f_im"], "c_fw2f_im")]
    fw2i_d = [p.const(cn["fw2i_re"], "c_fw2i_re"), p.const(cn["fw2i_imn"], "c_fw2i_imn")]
    wc_d = [p.const(cn[k], "c_" + k) for k in ("w1r", "w1i", "w1in", "v1r", "v1i", "v1in")]
    lv_d = p.const(np.broadcast_to(np.arange(2048, dtype=np.float32), (128, 2048)).copy(), "c_lvals")
    id_d = p.const(np.eye(128, dtype=BF), "c_idbf")
    W1R, W1I, W1IN, V1R, V1I, V1IN = range(6)

    with ExitStack() as es:
        w1s = es.enter_context(nc.sbuf_tensor("w1c", [128, 6 * 128], BF16))
        idb = es.enter_context(nc.sbuf_tensor("idb", [128, 128], BF16))
        prm = es.enter_context(nc.sbuf_tensor("prm", [128, 9 * 64 + 8], F32))
        sa = es.enter_context(nc.sbuf_tensor("sa", [128, 16384], BF16))
        sb = es.enter_context(nc.sbuf_tensor("sb2", [128, 16384], BF16))
        cslot = es.enter_context(nc.sbuf_tensor("cslot", [128, 16384], BF16))
        u1b = es.enter_context(nc.sbuf_tensor("u1b", [128, 16384], BF16))
        ysb = es.enter_context(nc.sbuf_tensor("ysb", [128, S], F32))
        ub = es.enter_context(nc.sbuf_tensor("ub", [128, S], BF16))
        pws = es.enter_context(nc.sbuf_tensor("pws", [128, 2048], F32))
        for i, cd in enumerate(wc_d):
            p.emit("sync", lambda e, i=i, cd=cd: e.dma_start(
                out=w1s[:, i * 128:(i + 1) * 128], in_=cd.ap()[:, :]),
                writes=("w1c",), dma=True)
        p.emit("sync", lambda e: e.dma_start(out=idb[:, :], in_=id_d.ap()[:, :]),
               writes=("idb",), dma=True)

        logdA = prm[:, 0:64]
        sgn = prm[:, 64:128]
        lnab = prm[:, 128:192]
        biasq = [prm[:, (3 + q) * 64:(4 + q) * 64] for q in range(4)]
        scr = prm[:, 7 * 64:8 * 64]
        dcol = prm[:, 8 * 64:8 * 64 + 1]
        Dcol = prm[:, 8 * 64 + 1:8 * 64 + 2]
        p.emit("sync", lambda e: e.dma_start(out=dcol, in_=dts[:, :]), writes=("prm",), dma=True)
        p.emit("sync", lambda e: e.dma_start(out=Dcol, in_=Dps[:, :]), writes=("prm",), dma=True)
        p.emit("sync", lambda e: e.dma_start(out=scr, in_=As[:, :]), writes=("prm",), dma=True)
        p.emit("scalar", lambda e: e.activation(out=logdA, in_=scr, func=AF.Exp, scale=-1.0),
               reads=("prm",), writes=("prm",))
        p.emit("vector", lambda e: e.tensor_scalar(logdA, logdA, dcol, -1.0, ALU.mult, ALU.mult),
               reads=("prm",), writes=("prm",))
        p.emit("sync", lambda e: e.dma_start(out=sgn, in_=Bs[:, :]), writes=("prm",), dma=True)
        p.emit("sync", lambda e: e.dma_start(out=lnab, in_=Cs[:, :]), writes=("prm",), dma=True)
        p.emit("gpsimd", lambda e: e.tensor_mul(scr, sgn, lnab), reads=("prm",), writes=("prm",))
        p.emit("vector", lambda e: e.tensor_scalar(scr, scr, dcol, None, ALU.mult),
               reads=("prm",), writes=("prm",))
        p.emit("scalar", lambda e: e.sign(sgn, scr), reads=("prm",), writes=("prm",))
        p.emit("scalar", lambda e: e.activation(out=lnab, in_=scr, func=AF.Abs),
               reads=("prm",), writes=("prm",))
        p.emit("scalar", lambda e: e.activation(out=lnab, in_=lnab, func=AF.Ln),
               reads=("prm",), writes=("prm",))
        for q in range(1, 4):
            p.emit("vector", lambda e, q=q: e.tensor_scalar(
                biasq[q], logdA, 2048.0 * q, None, ALU.mult),
                reads=("prm",), writes=("prm",))
            p.emit("gpsimd", lambda e, q=q: e.tensor_add(biasq[q], biasq[q], lnab),
                   reads=("prm",), writes=("prm",))

        kacc = pws[:, 0:2048]
        escr = ysb[:, 0:2048]
        lvs = ysb[:, 2048:4096]
        p.emit("sync", lambda e: e.dma_start(out=lvs, in_=lv_d.ap()[:, :]),
               writes=("ysb",), dma=True)
        for q in range(4):
            bias_ap = lnab if q == 0 else biasq[q]
            p.emit("vector", lambda e: e.memset(kacc, 0.0),
                   reads=("pws",), writes=("pws",))
            for n in range(64):
                p.emit("scalar", lambda e, n=n, bias_ap=bias_ap: e.activation(
                    out=escr, in_=lvs, func=AF.Exp,
                    scale=logdA[:, n:n + 1], bias=bias_ap[:, n:n + 1]),
                    reads=("ysb", "prm"), writes=("ysb",))
                p.emit("vector", lambda e, n=n: e.tensor_scalar(
                    escr, escr, sgn[:, n:n + 1], None, ALU.mult),
                    reads=("ysb", "prm"), writes=("ysb",))
                p.emit("gpsimd", lambda e: e.tensor_add(kacc, kacc, escr),
                       reads=("pws", "ysb"), writes=("pws",))
            p.emit("vector", lambda e, q=q: e.tensor_copy(
                ub[:, q * 2048:(q + 1) * 2048], kacc),
                reads=("pws", "ub"), writes=("ub",))
        p.emit("sync", lambda e: e.dma_start(out=kdram[:, :], in_=ub[:, :8192]),
               reads=("ub",), writes=("kd",), dma=True)
        p.barrier()

        if nph <= 1:
            p.emit("sync", lambda e: e.dma_start(out=yout[:, 0:8192], in_=ub[:, :]),
                   reads=("ub",), writes=("yo",), dma=True)
            p.finish(("yo",))
            return p.build()

        def phase(kphase, bidx):
            tg = "k" if kphase else "b%d" % bidx
            if kphase:
                src3 = kdram.rearrange("d (n j) -> n d j", j=128)
                p.emit("sync", lambda e: e.dma_start(
                    out=u1b[0:64, :].rearrange("p (d j) -> p d j", j=128), in_=src3),
                    reads=("kd", "u1b"), writes=("u1b",), dma=True)
                p.emit("sync", lambda e: e.dma_start(
                    out=u1b[64:128, :].rearrange("p (d j) -> p d j", j=128), in_=src3),
                    reads=("kd",), writes=("u1b",), dma=True)
            else:
                p.emit("sync", lambda e, bidx=bidx: e.dma_start(
                    out=u1b[0:64, :], in_=u1[:, bidx * 16384:(bidx + 1) * 16384]),
                    writes=("u1b",), dma=True)
                p.emit("sync", lambda e, bidx=bidx: e.dma_start(
                    out=u1b[64:128, :], in_=u1[:, bidx * 16384:(bidx + 1) * 16384]),
                    writes=("u1b",), dma=True)
                p.emit("sync", lambda e, bidx=bidx: e.dma_start(
                    out=ub[:, :], in_=ubd[:, bidx * S:(bidx + 1) * S]),
                    reads=("ub",), writes=("ub",), dma=True)
                for qq in range(4):
                    p.emit("vector", lambda e, qq=qq: e.tensor_scalar(
                        pws[:, 0:2048], ub[:, qq * 2048:(qq + 1) * 2048], Dcol, None, ALU.mult),
                        reads=("ub", "prm", "pws"), writes=("pws",))
                    p.emit("sync", lambda e, qq=qq: e.dma_start(
                        out=dpudram[:, qq * 2048:(qq + 1) * 2048], in_=pws[:, 0:2048]),
                        reads=("pws",), writes=("dpud",), dma=True)
            p.barrier()
            for half in range(2):
                for ci in range(2):
                    p.emit("sync", lambda e, ci=ci: e.dma_start(
                        out=cslot[64 * ci:64 * ci + 64, :], in_=fw2f_d[ci].ap()[:, :]),
                        reads=("cslot",), writes=("cslot",), dma=True)
                with nc.psum_tensor("p1r%s%d" % (tg, half), [64, 128], F32) as p1r, \
                     nc.psum_tensor("p1i%s%d" % (tg, half), [64, 128], F32) as p1i:
                    for j in range(128):
                        for ci in range(2):
                            ps = (p1r, p1i)[ci]
                            base = 64 * ci
                            if kphase:
                                lh = u1b[base:base + 64, 64 * half * 128 + j:64 * half * 128 + j + 63 * 128 + 1:128]
                            else:
                                lh = u1b[base:base + 64, j * 128 + 64 * half:j * 128 + 64 * half + 64]
                            p.emit("tensor", lambda e, lh=lh, ci=ci, j=j, ps=ps, base=base: e.matmul(
                                ps[:, :], lh, cslot[base:base + 64, j * 128:(j + 1) * 128],
                                start=True, stop=True),
                                reads=("u1b", "cslot"), writes=("p1_%d" % ci,))
                            p.emit("scalar", lambda e, ci=ci, j=j, ps=ps: e.activation(
                                out=sa[64 * ci:64 * ci + 64, j * 128:(j + 1) * 128],
                                in_=ps[:, :], func=AF.Copy),
                                reads=("p1_%d" % ci, "sa"), writes=("sa",))
                p.barrier()
                with nc.psum_tensor("pt1%s%d" % (tg, half), [128, 64], BF16) as pt1:
                    for ci in range(2):
                        for k2 in range(128):
                            p.emit("tensor", lambda e, ci=ci, k2=k2: e.transpose(
                                pt1[:, :], sa[64 * ci:64 * ci + 64, k2:k2 + 127 * 128 + 1:128],
                                idb[64 * ci:64 * ci + 64, 64 * ci:64 * ci + 64]),
                                reads=("sa", "idb"), writes=("pt1",))
                            p.emit("scalar", lambda e, ci=ci, k2=k2: e.activation(
                                out=sb[:, ci * 8192 + k2 * 64:ci * 8192 + k2 * 64 + 64],
                                in_=pt1[:, :], func=AF.Copy),
                                reads=("pt1", "sb"), writes=("sb",))
                p.barrier()
                if not kphase and nph <= 3:
                    continue
                if not kphase:
                    for ci in range(2):
                        p.emit("sync", lambda e, ci=ci, half=half: e.dma_start(
                            out=cslot[:, ci * 8192:(ci + 1) * 8192],
                            in_=kfdram[:, half * 16384 + ci * 8192:half * 16384 + (ci + 1) * 8192]),
                            reads=("kf", "cslot"), writes=("cslot",), dma=True)
                with nc.psum_tensor("pxr%s%d" % (tg, half), [128, 512], F32) as pxr, \
                     nc.psum_tensor("pxi%s%d" % (tg, half), [128, 512], F32) as pxi, \
                     nc.psum_tensor("per%s%d" % (tg, half), [128, 512], F32) as per_, \
                     nc.psum_tensor("pei%s%d" % (tg, half), [128, 512], F32) as pei:
                    for ch in range(16):
                        csl = slice(ch * 512, (ch + 1) * 512)
                        for (wi, si, pp, nm, st) in ((W1R, 0, pxr, "pxr", True), (W1IN, 1, pxr, "pxr", False),
                                                     (W1I, 0, pxi, "pxi", True), (W1R, 1, pxi, "pxi", False)):
                            p.emit("tensor", lambda e, wi=wi, si=si, pp=pp, st=st, csl=csl: e.matmul(
                                pp[:, :], w1s[:, wi * 128:(wi + 1) * 128],
                                sb[:, si * 8192:(si + 1) * 8192][:, csl],
                                start=st, stop=not st),
                                reads=("sb", "w1c"), writes=(nm,))
                        for ci, pp, nm in ((0, pxr, "pxr"), (1, pxi, "pxi")):
                            p.emit("scalar", lambda e, ci=ci, pp=pp, csl=csl: e.activation(
                                out=sa[:, ci * 8192:(ci + 1) * 8192][:, csl], in_=pp[:, :], func=AF.Copy),
                                reads=(nm, "sa"), writes=("sa",))
                        if not kphase and nph >= 4:
                            for q, (a_, b_) in enumerate(((0, 0), (1, 1), (0, 1), (1, 0))):
                                p.emit("gpsimd", lambda e, q=q, a_=a_, b_=b_, csl=csl: e.tensor_mul(
                                    pws[:, q * 512:(q + 1) * 512],
                                    sa[:, a_ * 8192:(a_ + 1) * 8192][:, csl],
                                    cslot[:, b_ * 8192:(b_ + 1) * 8192][:, csl]),
                                    reads=("sa", "cslot", "pws"), writes=("pws",))
                            p.emit("gpsimd", lambda e: e.tensor_sub(
                                pws[:, 0:512], pws[:, 0:512], pws[:, 512:1024]),
                                reads=("pws",), writes=("pws",))
                            p.emit("gpsimd", lambda e: e.tensor_add(
                                pws[:, 1024:1536], pws[:, 1024:1536], pws[:, 1536:2048]),
                                reads=("pws",), writes=("pws",))
                            for ci in range(2):
                                p.emit("scalar", lambda e, ci=ci, csl=csl: e.activation(
                                    out=sa[:, ci * 8192:(ci + 1) * 8192][:, csl],
                                    in_=pws[:, ci * 1024:ci * 1024 + 512], func=AF.Copy),
                                    reads=("pws", "sa"), writes=("sa",))
                            for (wi, si, pp, nm, st) in ((V1R, 0, per_, "per", True), (V1IN, 1, per_, "per", False),
                                                         (V1I, 0, pei, "pei", True), (V1R, 1, pei, "pei", False)):
                                p.emit("tensor", lambda e, wi=wi, si=si, pp=pp, st=st, csl=csl: e.matmul(
                                    pp[:, :], w1s[:, wi * 128:(wi + 1) * 128],
                                    sa[:, si * 8192:(si + 1) * 8192][:, csl],
                                    start=st, stop=not st),
                                    reads=("sa", "w1c"), writes=(nm,))
                            for ci, pp, nm in ((0, per_, "per"), (1, pei, "pei")):
                                p.emit("scalar", lambda e, ci=ci, pp=pp, csl=csl: e.activation(
                                    out=sb[:, ci * 8192:(ci + 1) * 8192][:, csl], in_=pp[:, :], func=AF.Copy),
                                    reads=(nm, "sb"), writes=("sb",))
                if kphase:
                    p.emit("sync", lambda e, half=half: e.dma_start(
                        out=kfdram[:, half * 16384:(half + 1) * 16384], in_=sa[:, :]),
                        reads=("sa",), writes=("kf",), dma=True)
                    p.barrier()
                    continue
                p.barrier()
                if nph <= 4:
                    continue
                with nc.psum_tensor("pt2%s%d" % (tg, half), [128, 128], BF16) as pt2:
                    for ci in range(2):
                        for dd in range(64):
                            p.emit("tensor", lambda e, ci=ci, dd=dd: e.transpose(
                                pt2[:, :], sb[:, ci * 8192 + dd:ci * 8192 + dd + 127 * 64 + 1:64],
                                idb[:, :]),
                                reads=("sb", "idb"), writes=("pt2",))
                            p.emit("scalar", lambda e, ci=ci, dd=dd: e.activation(
                                out=sa[:, ci * 8192 + dd * 128:ci * 8192 + (dd + 1) * 128],
                                in_=pt2[:, :], func=AF.Copy),
                                reads=("pt2", "sa"), writes=("sa",))
                p.barrier()
                if nph <= 5.1:
                    continue
                for ci in range(2):
                    p.emit("sync", lambda e, ci=ci: e.dma_start(
                        out=cslot[:, ci * 8192:(ci + 1) * 8192], in_=fw2i_d[ci].ap()[:, :]),
                        reads=("cslot",), writes=("cslot",), dma=True)
                with nc.psum_tensor("py2%s%d" % (tg, half), [64, 64], F32) as py2:
                    for j in range(128):
                        for ci in range(2):
                            p.emit("tensor", lambda e, ci=ci, j=j: e.matmul(
                                py2[:, :], sa[:, ci * 8192 + j:ci * 8192 + j + 63 * 128 + 1:128],
                                cslot[:, ci * 8192 + j * 64:ci * 8192 + (j + 1) * 64],
                                start=(ci == 0), stop=(ci == 1)),
                                reads=("sa", "cslot"), writes=("py2",))
                        p.emit("scalar", lambda e, j=j, half=half: e.activation(
                            out=ysb[64 * half:64 * half + 64, j:j + 63 * 128 + 1:128],
                            in_=py2[:, :], func=AF.Copy),
                            reads=("py2", "ysb"), writes=("ysb",))
                p.barrier()
            if not kphase and nph >= 5.2:
                if nph < 6:
                    p.emit("gpsimd", lambda e, bidx=bidx: e.dma_start(
                        out=yout[:, bidx * S:(bidx + 1) * S], in_=ysb[:, :]),
                        reads=("ysb",), writes=("yo",), dma=True)
                    p.barrier()
                    return

                p.emit("vector", lambda e: e.memset(ysb[:, :], 0.0), reads=("ysb",), writes=("ysb",)) if nph <= 5 else None
                for qq in range(8):
                    p.emit("gpsimd", lambda e, qq=qq: e.dma_start(
                        out=ysb[:, qq * 1024:(qq + 1) * 1024],
                        in_=dpudram[:, qq * 1024:(qq + 1) * 1024], accum_op=ALU.add),
                        reads=("dpud", "ysb"), writes=("ysb",), dma=True)
                p.emit("gpsimd", lambda e, bidx=bidx: e.dma_start(
                    out=yout[:, bidx * S:(bidx + 1) * S], in_=ysb[:, :]),
                    reads=("ysb",), writes=("yo",), dma=True)
                p.barrier()

        phase(True, 0)
        if nph >= 3:
            phase(False, 0)
            phase(False, 1)
        if nph >= 5.2:
            p.finish(("yo",))
        elif False:
            p.emit("sync", lambda e: e.dma_start(out=yout[:, 0:8192], in_=ub[:, :]),
                   reads=("ub", "kf"), writes=("yo",), dma=True)
            p.finish(("yo",))
    return p.build()



_PROGS = {}


def _prog(name, builder):
    if name not in _PROGS:
        _PROGS[name] = builder()
    return _PROGS[name]


def run(nc, in_maps):
    return run_bass_kernel_spmd(nc, in_maps, core_ids=list(range(NCORES))).results


def attention_stage(qkv_full):
    Q, K, V = qkv_full[0:D], qkv_full[D:2 * D], qkv_full[2 * D:3 * D]
    in_maps = []
    for c in range(NCORES):
        b = c // 4
        rows = slice(4 * (c % 4) * 64, (4 * (c % 4) + 4) * 64)
        cols = slice(b * SEQ, (b + 1) * SEQ)
        vt = np.zeros((SEQ, 260), BF)
        vv = vt.reshape(SEQ, 4, 65)
        vv[:, :, 0:64] = np.asarray(V[rows, cols]).T.reshape(SEQ, 4, 64)
        vv[:, :, 64] = np.ones((), BF)
        in_maps.append({
            "q": np.ascontiguousarray(Q[rows, cols]),
            "k": np.ascontiguousarray(K[rows, cols]),
            "v": vt,
        })
    r = run(_prog("p2", build_p2), in_maps)
    att_full = np.zeros((D, 2 * SEQ), BF)
    for c in range(NCORES):
        b = c // 4
        att_full[4 * (c % 4) * 64:(4 * (c % 4) + 4) * 64, b * SEQ:(b + 1) * SEQ] = r[c]["att"]
    return [np.ascontiguousarray(att_full[:, 2048 * c:2048 * (c + 1)]) for c in range(NCORES)]


def l0_ffn_stage(xsh, ins):
    ncp = _prog("p1", lambda: build_proj(3 * D, True, False, "p1"))
    r = run(ncp, [{"x": xsh[c], "wq": ins["L0_ffn_w_qkv"]} for c in range(NCORES)])
    xf = [r[c]["xf"] for c in range(NCORES)]
    qkv_full = np.concatenate([np.asarray(r[c]["proj"]) for c in range(NCORES)], axis=1)
    att = attention_stage(qkv_full)
    gb = np.stack([ins["L0_ffn_g1"], ins["L0_ffn_b1"], ins["L0_ffn_g2"], ins["L0_ffn_b2"]],
                  axis=1).astype(np.float32)
    gb_dev = np.ascontiguousarray(gb.reshape(ND, 128, 4).transpose(1, 0, 2).reshape(128, ND * 4))
    bb1_dev = np.ascontiguousarray(
        ins["L0_ffn_bb1"].reshape(8, 32, 128).transpose(0, 2, 1).reshape(8 * 128, 32))
    r = run(_prog("p3", build_moe_tail), [{
        "xres": xsh[c], "xf": xf[c], "att": att[c],
        "wo": ins["L0_ffn_w_o"], "gb": gb_dev, "wg": ins["L0_ffn_wg"],
        "w1": ins["L0_ffn_w1"].reshape(8 * D, 4096), "bb1": bb1_dev,
        "w2": ins["L0_ffn_w2"].reshape(8 * 4096, D), "bb2": ins["L0_ffn_bb2"],
    } for c in range(NCORES)])
    return [r[c]["xout"] for c in range(NCORES)]


def l0_attn_stage(xsh, ins):
    wq = np.ascontiguousarray(np.concatenate(
        [np.asarray(ins["L0_attn_q_proj"]).T, np.asarray(ins["L0_attn_k_proj"]).T,
         np.asarray(ins["L0_attn_v_proj"]).T], axis=1).astype(np.float32))
    ncp = _prog("p1", lambda: build_proj(3 * D, True, False, "p1"))
    r = run(ncp, [{"x": xsh[c], "wq": wq} for c in range(NCORES)])
    qkv_full = np.concatenate([np.asarray(r[c]["proj"]) for c in range(NCORES)], axis=1)
    att = attention_stage(qkv_full)
    woT = np.ascontiguousarray(np.asarray(ins["L0_attn_o_proj"]).T.astype(np.float32))
    ncp = _prog("p3b", lambda: build_proj_residual(False, 1.0, "p3b"))
    r = run(ncp, [{"xres": xsh[c], "srcb": att[c], "w": woT} for c in range(NCORES)])
    return [r[c]["xout"] for c in range(NCORES)]


def tile128(v):
    v = np.asarray(v, np.float32).reshape(-1, 128)
    return np.ascontiguousarray(v.T)


def l1_ffn_stage(xsh, ins):
    wrT = np.ascontiguousarray(np.asarray(ins["L1_ffn_Wr"]).T.astype(np.float32))
    r = run(_prog("p4", build_router), [{"x": xsh[c], "wr": wrT} for c in range(NCORES)])
    xnb = [r[c]["xnb"] for c in range(NCORES)]
    wt_full = np.ascontiguousarray(np.concatenate([np.asarray(r[c]["wt"]) for c in range(NCORES)], axis=0))
    r = run(_prog("p5", build_masks), [{"wt": wt_full}] * NCORES)
    gates = np.asarray(r[0]["gates"])
    fbv = np.asarray(r[0]["fb"])
    b1_dev = np.ascontiguousarray(
        np.asarray(ins["L1_ffn_b1"]).reshape(8, 32, 128).transpose(0, 2, 1).reshape(8 * 128, 32))
    im = []
    for c in range(NCORES):
        im.append({
            "xres": xsh[c], "xnb": xnb[c],
            "gates": np.ascontiguousarray(gates[2048 * c:2048 * (c + 1)]),
            "fbv": np.ascontiguousarray(fbv[2048 * c:2048 * (c + 1)]),
            "fbt": np.ascontiguousarray(fbv[2048 * c:2048 * (c + 1)].reshape(16, 128).T),
            "w1": np.asarray(ins["L1_ffn_W1"]).reshape(8 * D, 4096),
            "bb1": b1_dev,
            "w2": np.asarray(ins["L1_ffn_W2"]).reshape(8 * 4096, D),
            "bb2": np.asarray(ins["L1_ffn_b2"]),
            "fw1": np.asarray(ins["L1_ffn_fw1"]),
            "fb1": tile128(ins["L1_ffn_fb1"]),
            "fw2": np.asarray(ins["L1_ffn_fw2"]),
            "fb2": np.asarray(ins["L1_ffn_fb2"]).reshape(1, D),
        })
    r = run(_prog("p6", build_moe2), im)
    return [r[c]["xout"] for c in range(NCORES)]


def l1_attn_stage(xsh, ins):
    winT = np.ascontiguousarray(np.asarray(ins["L1_attn_Win"]).T.astype(np.float32))
    binT = tile128(np.asarray(ins["L1_attn_bin_"]))
    ncp = _prog("p7", lambda: build_proj(D, False, True, "p7"))
    r = run(ncp, [{"x": xsh[c], "wq": winT, "bvec": binT} for c in range(NCORES)])
    u_fm = np.concatenate([np.asarray(r[c]["proj"]) for c in range(NCORES)], axis=1)
    u4 = u_fm.reshape(D, 2, 64, 128)
    im = []
    for c in range(NCORES):
        blk = slice(128 * c, 128 * (c + 1))
        u1 = np.ascontiguousarray(u4[blk].transpose(2, 1, 3, 0).reshape(64, 2 * 16384))
        im.append({
            "u1": u1,
            "ubd": np.ascontiguousarray(u_fm[blk]),
            "As": np.ascontiguousarray(np.asarray(ins["L1_attn_A_log"], np.float32)[blk]),
            "dts": np.ascontiguousarray(np.asarray(ins["L1_attn_dt"], np.float32)[blk, None]),
            "Bs": np.ascontiguousarray(np.asarray(ins["L1_attn_B"], np.float32)[blk]),
            "Cs": np.ascontiguousarray(np.asarray(ins["L1_attn_C"], np.float32)[blk]),
            "Dps": np.ascontiguousarray(np.asarray(ins["L1_attn_Dp"], np.float32)[blk, None]),
        })
    r = run(_prog("p8", build_p8), im)
    y_fm = np.concatenate([np.asarray(r[c]["yout"]) for c in range(NCORES)], axis=0)
    woutT = np.ascontiguousarray(np.asarray(ins["L1_attn_Wout"]).T.astype(np.float32))
    boutT = tile128(np.asarray(ins["L1_attn_bout"]))[:, 0:ND]
    ncp = _prog("p9", lambda: build_proj_residual(True, 1.0, "p9"))
    r = run(ncp, [{"xres": xsh[c], "srcb": np.ascontiguousarray(y_fm[:, 2048 * c:2048 * (c + 1)]),
                   "w": woutT, "bvec": boutT} for c in range(NCORES)])
    return [r[c]["xout"] for c in range(NCORES)]


def kernel(**inputs):
    x = np.asarray(inputs["L0_attn_X"], np.float32).reshape(2 * SEQ, D)
    xsh = shard_fm(x)
    xsh = l0_ffn_stage(xsh, inputs)
    xsh = l0_attn_stage(xsh, inputs)
    xsh = l0_ffn_stage(xsh, inputs)
    xsh = l1_ffn_stage(xsh, inputs)
    xsh = l1_attn_stage(xsh, inputs)
    xsh = l1_ffn_stage(xsh, inputs)
    return unshard_fm(xsh).reshape(2, SEQ, D)
